# version 16 (final)
# baseline (speedup 1.0000x reference)
import numpy as np
import concourse.bass as bass
import concourse.mybir as mybir
from concourse.bass import IndirectOffsetOnAxis
from concourse.bass_utils import run_bass_kernel_spmd

F32 = mybir.dt.float32
BF16 = mybir.dt.bfloat16
U32 = mybir.dt.uint32
AF = mybir.ActivationFunctionType
ALU = mybir.AluOpType
AX = mybir.AxisListType

B, N, D = 8, 2048, 1024
H, HD = 16, 64
INNER = H * HD
E, TOPK, CAP = 64, 6, 512
I_FF = 704
SI = 1408
NC_ = 8
NT = N // 128
EPC = E // NC_
EXP_C = 4.0
SBUF_BYTES = 212736

ENGINES = ("sync", "scalar", "vector", "tensor", "gpsimd")


class Tok:
    __slots__ = ("sem", "val")

    def __init__(self, sem, val):
        self.sem = sem
        self.val = val


class Buf:
    __slots__ = ("w", "r", "name")

    def __init__(self, name=""):
        self.w = []
        self.r = []
        self.name = name

    def toks(self):
        return self.w + self.r


class Ctx:
    def __init__(self):
        self.rec = {e: [] for e in ENGINES}
        self.cnt = {e: 0 for e in ENGINES}
        self.watermark = {e: {} for e in ENGINES}

    def _wait(self, eng, sem, val):
        if sem == "tensor" and eng == "tensor":
            return
        if self.watermark[eng].get(sem, 0) >= val:
            return
        self.watermark[eng][sem] = val
        self.rec[eng].append(("wait", sem, val))

    def op(self, eng, emit, ins=(), outs=(), extra=(), inc=1, tag="", sem=None):
        best = {}
        for b in ins:
            for t in b.w:
                if best.get(t.sem, 0) < t.val:
                    best[t.sem] = t.val
        for b in outs:
            for t in b.w + b.r:
                if best.get(t.sem, 0) < t.val:
                    best[t.sem] = t.val
        for t in extra:
            if t is not None and best.get(t.sem, 0) < t.val:
                best[t.sem] = t.val
        for s, v in sorted(best.items()):
            self._wait(eng, s, v)
        s_ = sem if sem is not None else eng
        self.cnt[s_] = self.cnt.get(s_, 0) + inc
        tok = Tok(s_, self.cnt[s_])
        self.rec[eng].append(("instr", emit, s_, inc, tag))
        for b in ins:
            b.r.append(tok)
        for b in outs:
            b.w = [tok]
            b.r = []
        return tok

    def finalize_group(self, sem, *bufs):
        tok = Tok(sem, self.cnt.get(sem, 0))
        for b in bufs:
            b.w = [tok]
            b.r = []

    def fence(self, *bufs):
        out = []
        for b in bufs:
            out.extend(b.w)
            out.extend(b.r)
        return out

    def replay(self, eng, proxy, sems):
        for item in self.rec[eng]:
            if item[0] == "wait":
                proxy.wait_ge(sems[item[1]], item[2])
            else:
                _, emit, seng, inc, tag = item
                emit(proxy).then_inc(sems[seng], inc)


def build_program(tap=None, stop_after=None):
    nc = bass.Bass("TRN2", target_bir_lowering=False)

    def ein(name, shape, dt=F32):
        return nc.dram_tensor(name, shape, dt, kind="ExternalInput").ap()

    x_in = ein("x", [N, D])
    wqkv = ein("wqkv", [D, 3 * INNER])
    wout = ein("wout", [INNER, D])
    boutc_i = ein("boutc", [128, 8])
    gamma_i = ein("gamma", [1, D])
    beta_i = ein("beta", [1, D])
    gwT = ein("gwT", [D, E])
    gateb_i = ein("gateb", [1, E])
    ws13 = ein("ws13", [D, 2 * SI])
    ws2T_i = ein("ws2T", [SI, D])
    ew13 = ein("ew13", [EPC, D, 2 * I_FF])
    ew2T = ein("ew2T", [EPC, 768, D])
    cosf_i = ein("cosf", [128, N])
    sins_i = ein("sins", [128, N])
    identb_i = ein("identb", [128, 128], BF16)
    permf_i = ein("permf", [128, 128])
    identf_i = ein("identf", [128, 128])
    trix_i = ein("trix", [128, 128])
    onesr_i = ein("onesr", [1, 128])
    onesc_i = ein("onesc", [128, 1])
    iota64_i = ein("iota64", [128, E])
    gbase_i = ein("gbase", [128, E])
    tokid_i = ein("tokid", [128, NT])
    sel_i = ein("sel", [128, 16])
    slotrow_i = ein("slotrow", [32, 1], U32)

    out = nc.dram_tensor("out", [N, D], F32, kind="ExternalOutput").ap()

    x1_mine = nc.dram_tensor("x1_mine", [N, D], BF16).ap()
    x1_all = nc.dram_tensor("x1_all", [B * N, D], BF16).ap()
    cnt_mine = nc.dram_tensor("cnt_mine", [NT, E], F32).ap()
    cnt_all = nc.dram_tensor("cnt_all", [NC_ * NT, E], F32).ap()
    slot_part = nc.dram_tensor("slot_part", [E * CAP, 1], F32).ap()
    slot_all = nc.dram_tensor("slot_all", [E * CAP, 1], F32).ap()
    yb_mine = nc.dram_tensor("yb_mine", [EPC * CAP, D], BF16).ap()
    ew13b = nc.dram_tensor("ew13b", [EPC, D, 2 * I_FF], BF16).ap()
    ew2b = nc.dram_tensor("ew2b", [EPC, 768, D], BF16).ap()
    ws13b = nc.dram_tensor("ws13b", [1024, 2816], BF16).ap()
    dden = nc.dram_tensor("dden", [1, 1024], F32).ap()
    ws2b = nc.dram_tensor("ws2b", [1408, 1024], BF16).ap()
    gco_dram = nc.dram_tensor("gco_dram", [16, 64], F32).ap()
    yb_all = nc.dram_tensor("yb_all", [E * CAP, D], BF16).ap()

    taps = {}
    if tap == "x1":
        taps["t_x1"] = nc.dram_tensor("t_x1", [N, D], F32, kind="ExternalOutput").ap()
    if tap == "route":
        taps["t_rwk"] = nc.dram_tensor("t_rwk", [128, 128], F32, kind="ExternalOutput").ap()
        taps["t_raddr"] = nc.dram_tensor("t_raddr", [128, 128], U32, kind="ExternalOutput").ap()
        taps["t_ridxf"] = nc.dram_tensor("t_ridxf", [128, 128], F32, kind="ExternalOutput").ap()
        taps["t_slot"] = nc.dram_tensor("t_slot", [E * CAP, 1], F32, kind="ExternalOutput").ap()
    if tap == "expert":
        taps["t_yb"] = nc.dram_tensor("t_yb", [EPC * CAP, D], BF16, kind="ExternalOutput").ap()
        taps["t_sidx"] = nc.dram_tensor("t_sidx", [128, 32], F32, kind="ExternalOutput").ap()

    c = Ctx()

    class _StopBuild(Exception):
        pass

    def _maybe_stop(name):
        if stop_after == name:
            raise _StopBuild()

    AR_COLS = SBUF_BYTES // 4
    arena_g = nc.alloc_sbuf_tensor("arena", [128, AR_COLS], F32, align_bytes=64)
    A_ = arena_g.ap()
    cursor = [0]

    def carve(cols, dt=F32, at=None, part=128):
        elem = 4 if dt in (F32, U32) else 2
        if at is None:
            off = cursor[0]
            cursor[0] = off + ((cols * elem + 63) // 64) * 64
        else:
            off = at
        assert off % 4 == 0 and off + cols * elem <= SBUF_BYTES, (off, cols, elem)
        f32cols = (cols * elem + 3) // 4
        apv = A_[0:part, off // 4: off // 4 + f32cols]
        if dt == BF16:
            apv = apv.bitcast(BF16)
        elif dt == U32:
            apv = apv.bitcast(U32)
        return apv

    wk = carve(4096)
    stat = carve(64)
    amat = carve(NT * E)
    rwk = carve(NT * 8)
    ridxf = carve(NT * 8)
    rtop = carve(16)
    scnt = carve(E)
    sgco = carve(E)
    gcof = carve(NT * E)
    gateb_b = carve(E)
    boutc = carve(8)
    iota64 = carve(E)
    gbase = carve(E)
    tokid = carve(NT)
    onesr = carve(128)
    onesc = carve(16)
    trix = carve(128)
    identf = carve(128)
    identb = carve(128, BF16)
    permf = carve(128)
    gb1 = carve(E)
    sel = carve(16)
    slotrow = carve(32, U32)
    gw_b = carve(8 * E, BF16)
    ridx = carve(16, U32)
    sidx = carve(32, U32)
    sidxf = carve(32)
    raddr = carve(NT * 8, U32)
    raddr2 = carve(NT * 8, U32)
    slotsf = carve(128)
    zrow = carve(256)
    epsc = carve(16)
    ones2 = carve(64)
    DYN = ((cursor[0] + 63) // 64) * 64
    assert DYN <= 44000, DYN

    xnT = carve(8 * N, BF16, at=DYN + 0)
    wq_db = carve(2 * 1024, BF16, at=DYN + 32768)
    qT = carve(8 * N, BF16, at=DYN + 36864)
    kT = carve(8 * N, BF16, at=DYN + 69632)
    vA = carve(NT * H * 65, BF16, at=DYN + 102400)
    oT = qT
    cosf = carve(N, F32, at=DYN + 102400)
    sins = carve(N, F32, at=DYN + 110592)
    rsc = carve(3 * 1024, F32, at=DYN + 118784)
    gamma_b = carve(D, F32, at=DYN + 36864)
    beta_b = carve(D, F32, at=DYN + 40960)
    df1 = carve(D, F32, at=DYN + 69632)
    wkb = carve(4096, BF16, at=DYN + 135680)
    pcastb = carve(2 * 1408, BF16, at=DYN + 170752)
    shsilu = carve(1024, F32, at=DYN + 144128)
    ws2_b = carve(11 * D, BF16, at=DYN + 148224)
    assert DYN + 176384 <= SBUF_BYTES, (DYN, DYN + 176384)
    wo_b = carve(8 * D, BF16, at=DYN + 0)
    x1T = carve(8 * N, BF16, at=DYN + 0)
    yacc = carve(NT * D, F32, at=DYN + 69632)
    gsh = carve(11 * 1024, BF16, at=DYN + 32768)
    wshdb = carve(2 * 2048, BF16, at=DYN + 55296)
    ew2_db = carve(2 * 6 * D, BF16, at=DYN + 0)
    ew13_db = carve(2 * 2048, BF16, at=DYN + 24576)
    xbT = carve(8 * CAP, BF16, at=DYN + 32768)
    xbT1 = carve(8 * CAP, BF16, at=DYN + 59392)
    xbT2 = carve(8 * CAP, BF16, at=DYN + 135680)
    xbT3 = carve(8 * CAP, BF16, at=DYN + 148224)
    xbT4 = carve(8 * CAP, BF16, at=DYN + 156416)
    hT = carve(6 * CAP, BF16, at=DYN + 40960)
    xbt = carve(2 * 1024, BF16, at=DYN + 47104)
    xbtf = carve(1024, F32, at=DYN + 51200)
    ybs = carve(2 * 1024, BF16, at=DYN + 55296)
    gbuf = carve(4 * 1024, BF16, at=DYN + 0)

    pm1 = nc.alloc_psum_tensor("pm1", [128, 1024], F32).ap()
    pm2 = nc.alloc_psum_tensor("pm2", [128, 1024], F32).ap()
    ppv = nc.alloc_psum_tensor("ppv", [128, 2048], F32).ap()
    def pslot(s):
        return ppv[:, s * 512: s * 512 + 128]

    bufs = {}

    def V(name):
        if name not in bufs:
            bufs[name] = Buf(name)
        return bufs[name]

    pmA = [pm1, pm2]
    pmB = [V("pm1"), V("pm2")]
    ppvB = [V("ppvb0"), V("ppvb1"), V("ppvb2"), V("ppvb3")]

    _regcache = {}

    def bcr(e, v):
        key = v
        if key not in _regcache:
            r = e.alloc_register()
            e.reg_mov(r, v)
            _regcache[key] = r
        return _regcache[key]

    def dma(eng, dst, src, ins=(), outs=(), extra=(), tag="dma", sem=None):
        return c.op(eng, lambda e, d=dst, s=src: e.dma_start(out=d, in_=s),
                    ins=ins, outs=outs, extra=extra, inc=16, tag=tag,
                    sem=(sem if sem is not None else "q_" + tag))

    wkB = [(wk[:, i * 1024:(i + 1) * 1024], V(f"wkR{i}")) for i in range(4)]
    wkbB = [(wkb[:, i * 1024:(i + 1) * 1024], V(f"wkbR{i}")) for i in range(4)]
    stS = [(stat[:, i:i + 1], V(f"st{i}")) for i in range(8)]

    def sinv_ap(t):
        return stat[:, 16 + t:17 + t]

    for dst, src, nm in ((identb, identb_i, "identb"), (identf, identf_i, "identf"),
                         (trix, trix_i, "trix"), (onesr[0:1, 0:128], onesr_i, "onesr"),
                         (permf, permf_i, "permf"),
                         (onesc[:, 0:1], onesc_i, "onesc"), (iota64, iota64_i, "iota64"),
                         (gbase, gbase_i, "gbase"),
                         (tokid, tokid_i, "tokid"), (gb1[0:1, :], gateb_i, "gb1"),
                         (boutc, boutc_i, "boutc"), (sel, sel_i, "sel"),
                         (slotrow[0:32, 0:1], slotrow_i, "slotrow")):
        dma("sync", dst, src, outs=[V(nm)], tag="ldc")
    dma("sync", cosf, cosf_i, outs=[V("cosf")], tag="ldcg", sem="q_ldcg")
    dma("sync", sins, sins_i, outs=[V("sins")], tag="ldcg", sem="q_ldcg")
    dma("sync", wk[0:1, 0:1024], gamma_i, outs=[wkB[0][1]], tag="ldg")
    dma("sync", wk[0:1, 1024:2048], beta_i, outs=[wkB[1][1]], tag="ldb")
    c.finalize_group("q_ldc", *(V(nm) for nm in (
        "identb", "identf", "trix", "onesr", "permf", "onesc", "iota64", "gbase",
        "tokid", "gb1", "boutc", "sel", "slotrow")))
    c.finalize_group("q_ldcg", V("cosf"), V("sins"))

    def bcast_row(dst, dname, src_ap, src_buf, width, psum, pbuf):
        for off in range(0, width, 512):
            w = min(512, width - off)
            c.op("tensor", lambda e, o=off, w_=w, p=psum, s=src_ap: e.matmul(
                p[:, o:o + w_], onesr[0:1, :], s[:, o:o + w_], start=True, stop=True),
                ins=[src_buf, V("onesr")], outs=[pbuf], tag="bc")
        c.op("vector", lambda e, p=psum, d=dst, w_=width: e.tensor_copy(d[:, 0:w_], p[:, 0:w_]),
             ins=[pbuf], outs=[V(dname)])

    bcast_row(gamma_b, "gamma_b", wk[0:1, 0:1024], wkB[0][1], D, pm1, pmB[0])
    bcast_row(beta_b, "beta_b", wk[0:1, 1024:2048], wkB[1][1], D, pm2, pmB[1])
    bcast_row(gateb_b, "gateb_b", gb1[0:1, :], V("gb1"), E, pm1, pmB[0])

    for ic in range(8):
        dma("gpsimd", gw_b[:, ic * E:(ic + 1) * E], gwT[ic * 128:(ic + 1) * 128, :],
            outs=[V(f"gw_{ic}")], tag="ldgw")
    c.finalize_group("q_ldgw", *(V(f"gw_{ic}") for ic in range(8)))

    c.op("vector", lambda e: e.memset(zrow, 0.0), outs=[V("zrow")])
    c.op("vector", lambda e: e.memset(epsc[:, 0:1], 1e-5), outs=[V("epsc")])
    c.op("vector", lambda e: e.memset(ones2, 1.0), outs=[V("ones2")])
    c.op("vector", lambda e: e.memset(epsc[:, 1:2], -EXP_C), outs=[V("negc")])
    spart_b = V("slot_part")
    dma("gpsimd", slot_part.rearrange("(p f) one -> p (f one)", p=128),
        zrow[:, 0:256], ins=[V("zrow")], outs=[spart_b], tag="zslot")

    try:
        _maybe_stop("consts")
        pp_jobs = []
        for ch8 in range(8):
            for hh in range(2):
                pp_jobs.append(("s13", ch8, hh))
        for ch11 in range(11):
            pp_jobs.append(("s2", ch11, 0))
        for el in range(EPC):
            for ch8 in range(8):
                pp_jobs.append(("w13", el, ch8))
            for ch6 in range(6):
                pp_jobs.append(("w2", el, ch6))
        pcs_b = [V("pcs0"), V("pcs1")]
        pcast_b = V("pcast")
        pcsh_b = V("pcast_sh")

        def emit_prepass():
            i = 0
            for kind, el, chn in pp_jobs:
                sl = i % 2
                i += 1
                ob = pcast_b
                if kind == "w13":
                    wsrc = ew13[el][chn * 128:(chn + 1) * 128, :]
                    wdst = ew13b[el][chn * 128:(chn + 1) * 128, :]
                    wid = 1408
                elif kind == "w2":
                    wsrc = ew2T[el][chn * 128:(chn + 1) * 128, :]
                    wdst = ew2b[el][chn * 128:(chn + 1) * 128, :]
                    wid = 1024
                elif kind == "s13":
                    wsrc = ws13[el * 128:(el + 1) * 128, chn * 1408:(chn + 1) * 1408]
                    wdst = ws13b[el * 128:(el + 1) * 128, chn * 1408:(chn + 1) * 1408]
                    wid = 1408
                    ob = pcsh_b
                else:
                    wsrc = ws2T_i[el * 128:(el + 1) * 128, :]
                    wdst = ws2b[el * 128:(el + 1) * 128, :]
                    wid = 1024
                    ob = pcsh_b
                dma("gpsimd", pcastb[:, sl * 1408:sl * 1408 + wid], wsrc,
                    outs=[pcs_b[sl]], tag="ldpc", sem="q_pc%d" % sl)
                dma("gpsimd", wdst, pcastb[:, sl * 1408:sl * 1408 + wid],
                    ins=[pcs_b[sl]], outs=[ob], tag="stpc",
                    sem="q_pc3" if ob is pcsh_b else "q_pc2")

        xnT_b = [[V(f"xnT_{kc}_{it}") for it in range(NT)] for kc in range(8)]
        for it in range(NT if stop_after not in ("A1", "A2") else 2):
            par = it % 2
            xa, bxa = wkB[par]
            dma("sync" if par == 0 else "scalar", xa, x_in[it * 128:(it + 1) * 128, :],
                outs=[bxa], tag="ldx", sem="q_ldx%d" % par)
            sm, bsm = stat[:, 44 + par:45 + par], V(f"ln0sm{par}")
            c.op("vector", lambda e, a=xa, s=sm: e.reduce_sum(s, a, axis=AX.X),
                 ins=[bxa], outs=[bsm])
            c.op("vector", lambda e, s=sm: e.tensor_scalar_mul(s, s, 1.0 / D),
                 ins=[bsm], outs=[bsm])
            df, bdf = (wkB[2][0], V("ln0df0")) if par == 0 else (df1, V("ln0df1"))
            c.op("vector", lambda e, a=xa, d=df, s=sm: e.tensor_scalar(
                d, a, s, None, op0=ALU.subtract), ins=[bxa, bsm], outs=[bdf])
            sq, bsq = stat[:, 46 + par:47 + par], V(f"ln0sq{par}")
            dump, bdump = wkB[3]
            c.op("scalar", lambda e, d=df, du=dump, s=sq: e.activation(
                du, d, AF.Square, accum_out=s), ins=[bdf], outs=[bdump, bsq])
            sr, bsr = stat[:, 48 + par:49 + par], V(f"ln0sr{par}")
            c.op("scalar", lambda e, s=sq, r=sr: e.activation(r, s, AF.Sqrt, bias=epsc[:, 0:1], scale=1.0 / D),
                 ins=[bsq, V("epsc")], outs=[bsr])
            c.op("vector", lambda e, r=sr: e.reciprocal(r, r), ins=[bsr], outs=[bsr])
            c.op("vector", lambda e, d=df, r=sr: e.tensor_scalar(d, d, r, None, op0=ALU.mult),
                 ins=[bdf, bsr], outs=[bdf])
            c.op("vector", lambda e, d=df: e.tensor_mul(d, d, gamma_b),
                 ins=[bdf, V("gamma_b")], outs=[bdf])
            c.op("vector", lambda e, d=df: e.tensor_add(d, d, beta_b),
                 ins=[V("beta_b")], outs=[bdf])
            if stop_after == "A1":
                continue
            for dcj in range(8):
                sl = dcj % 4
                c.op("tensor", lambda e, d=dcj, s=sl, i_=df: e.transpose(
                    pslot(s), i_[:, d * 128:(d + 1) * 128], identf),
                    ins=[bdf, V("identf")], outs=[ppvB[sl]], tag="tpxn")
                c.op("vector", lambda e, d=dcj, s=sl, i=it: e.tensor_copy(
                    xnT[:, d * N + i * 128: d * N + i * 128 + 128], pslot(s)),
                    ins=[ppvB[sl]], outs=[xnT_b[dcj][it]])
        _maybe_stop("A")
        _maybe_stop("A1")
        _maybe_stop("A2")
        qT_b = [[V(f"qT_{ch}_{tf}") for tf in range(2)] for ch in range(8)]
        kT_b = [[V(f"kT_{ch}_{tf}") for tf in range(2)] for ch in range(8)]
        wqs = [V("wqs0"), V("wqs1")]

        for oc in range(16):
            slot = oc % 2
            src = wqkv[:, oc * 128:(oc + 1) * 128].rearrange("(k p) cc -> p k cc", p=128)
            dma("gpsimd", wq_db[:, slot * 1024:(slot + 1) * 1024], src,
                outs=[wqs[slot]], tag="ldwq", sem="q_wq%d" % slot)
            dst_sb = qT if oc < 8 else kT
            dst_bl = qT_b if oc < 8 else kT_b
            ch = oc % 8
            for tf in range(2):
                pb = pmB[tf]
                pa = pmA[tf]
                for hf in range(2):
                    for kc in range(8):
                        c.op("tensor", lambda e, k=kc, f=tf, s=slot, h2=hf, p=pa: e.matmul(
                            p[:, h2 * 512:h2 * 512 + 512],
                            wq_db[:, s * 1024 + k * 128: s * 1024 + k * 128 + 128],
                            xnT[:, k * N + f * 1024 + h2 * 512: k * N + f * 1024 + h2 * 512 + 512],
                            start=(k == 0), stop=(k == 7)),
                            ins=[wqs[slot]] + xnT_b[kc][8 * tf:8 * tf + 8],
                            outs=[pb], tag="qkv")
                sc_ = rsc[:, 0:1024]
                tmp = rsc[:, 1024:2048]
                u = rsc[:, 2048:3072]
                bsc, btmp, bu = V("rscA"), V("rscB"), V("rscC")
                c.op("scalar", lambda e, p=pa, s=sc_: e.activation(s, p[:, 0:1024], AF.Copy),
                     ins=[pb], outs=[bsc], tag="ropecast")
                for h2 in range(2):
                    c.op("tensor", lambda e, s=sc_, hh2=h2: e.matmul(
                        ppv[:, 1024 + hh2 * 512:1024 + hh2 * 512 + 512], permf,
                        s[:, hh2 * 512:hh2 * 512 + 512],
                        start=True, stop=True),
                        ins=[bsc, V("permf")], outs=[ppvB[2 + h2]], tag="ropeswap")
                c.op("vector", lambda e, s=sc_, t=tmp, f=tf: e.tensor_mul(
                    t, s, cosf[:, f * 1024:(f + 1) * 1024]),
                    ins=[bsc, V("cosf")], outs=[btmp])
                c.op("vector", lambda e, uu=u, f=tf: e.tensor_mul(
                    uu, ppv[:, 1024:2048], sins[:, f * 1024:f * 1024 + 1024]),
                    ins=[ppvB[2], ppvB[3], V("sins")], outs=[bu], tag="ropeu")
                c.op("vector", lambda e, t=tmp, uu=u, d=dst_sb, ch_=ch, f=tf: e.tensor_add(
                    d[:, ch_ * N + f * 1024: ch_ * N + f * 1024 + 1024], t, uu),
                    ins=[btmp, bu], outs=[dst_bl[ch][tf]], tag="rope")

        fence_wk = c.fence(*[b for _, b in wkB])
        vw = wk.bitcast(BF16)
        vw_b = [V(f"vw_{kc}") for kc in range(8)]
        for kc in range(8):
            dma("gpsimd", vw[:, kc * 1024:(kc + 1) * 1024],
                wqkv[kc * 128:(kc + 1) * 128, 2048:3072],
                outs=[vw_b[kc]], extra=fence_wk, tag="ldvw")
        c.finalize_group("q_ldvw", *vw_b)
        vA_b = [V(f"vA_{m}") for m in range(NT)]
        fence_rope = c.fence(*(qT_b[ch][tf] for ch in range(8) for tf in range(2)),
                             *(kT_b[ch][tf] for ch in range(8) for tf in range(2)))
        c.op("vector", lambda e: e.memset(vA, 1.0), outs=[V("vA_ones")], extra=fence_rope, tag="vones")
        for m in range(NT):
            pb = pmB[m % 2]
            pa = pmA[m % 2]
            for h2 in range(2):
                for kc in range(8):
                    c.op("tensor", lambda e, k=kc, mm=m, hh2=h2, p=pa: e.matmul(
                        p[:, hh2 * 512:hh2 * 512 + 512],
                        xnT[:, k * N + mm * 128: k * N + mm * 128 + 128],
                        vw[:, k * 1024 + hh2 * 512: k * 1024 + hh2 * 512 + 512],
                        start=(k == 0), stop=(k == 7)),
                        ins=[vw_b[kc], xnT_b[kc][m]], outs=[pb], tag="vmm")
            for h in range(H):
                c.op("vector", lambda e, hh=h, mm=m, p=pa: e.tensor_copy(
                    vA[:, mm * 1040 + hh * 65: mm * 1040 + hh * 65 + 64],
                    p[:, hh * 64:(hh + 1) * 64]),
                    ins=[pb], outs=([vA_b[m]] if h == H - 1 else []),
                    extra=list(V("vA_ones").w), tag="vevac")

        _maybe_stop("B")
        fence_xn_all = c.fence(*(xnT_b[kc][it2] for kc in range(8) for it2 in range(NT)))
        for ic in range(8):
            dma("gpsimd", wo_b[:, ic * D:(ic + 1) * D], wout[ic * 128:(ic + 1) * 128, :],
                outs=[V(f"wo_{ic}")], extra=fence_xn_all, tag="ldwo")
        c.finalize_group("q_ldwo", *(V(f"wo_{ic}") for ic in range(8)))
        emit_prepass()

        oT_b = [[V(f"oT_{h}_{qf}") for qf in range(2)] for h in range(H)]
        pm1hB = [V("pm1h0"), V("pm1h1")]
        pm2hB = [V("pm2h0"), V("pm2h1")]
        pAslot = [(wkb[:, s * 512:s * 512 + 512], V(f"PA{s}")) for s in range(3)]
        pBslot = [(wkb[:, 1536 + s * 512:1536 + s * 512 + 512], V(f"PB{s}")) for s in range(3)]
        btm = wkb[:, 3072:3584]
        bbtm = V("btmS")

        pairs = [(pm1, pm1hB[0]), (pm2, pm2hB[0])]
        pslotP = [(wkb[:, s * 1024:s * 1024 + 1024], V(f"PP{s}")) for s in range(3)]

        def emit_S(hp, qf4, kc, first):
            inq = [qT_b[hp][qf4 // 2], kT_b[hp][kc // 8]]
            pt, ptb = pairs[kc % 2]
            c.op("tensor", lambda e, k=kc, h_=hp, q4=qf4, d=pt: e.matmul(
                d[:, 0:512], kT[0:64, h_ * N + k * 128: h_ * N + k * 128 + 128],
                qT[0:64, h_ * N + q4 * 512: h_ * N + q4 * 512 + 512],
                start=True, stop=True, tile_position=(0, 0)),
                ins=inq, outs=[ptb], tag="sA",
                extra=(c.fence(pmB[0], pmB[1]) if first else ()))
            c.op("tensor", lambda e, k=kc, h_=hp, q4=qf4, d=pt: e.matmul(
                d[:, 512:1024], kT[64:128, h_ * N + k * 128: h_ * N + k * 128 + 128],
                qT[64:128, h_ * N + q4 * 512: h_ * N + q4 * 512 + 512],
                start=True, stop=True, tile_position=(64, 0)),
                ins=inq, outs=[ptb], tag="sB")

        def emit_exp(kc):
            sl = kc % 3
            pt, ptb = pairs[kc % 2]
            c.op("scalar", lambda e, d=pslotP[sl][0], s=pt: e.activation(
                d, s[:, 0:1024], AF.Exp, scale=0.125, bias=epsc[:, 1:2]),
                ins=[ptb, V("negc")], outs=[pslotP[sl][1]], tag="expAB")

        def emit_pv(hp, k1, cp):
            sl = k1 % 3
            base = cp * 1024
            c.op("tensor", lambda e, k=k1, h_=hp, d=pslotP[sl][0], b=base: e.matmul(
                ppv[0:65, b:b + 512],
                vA[:, k * 1040 + (2 * h_) * 65: k * 1040 + (2 * h_) * 65 + 65],
                d[:, 0:512], start=(k == 0), stop=(k == NT - 1)),
                ins=[pslotP[sl][1], vA_b[k1]], outs=[ppvB[2 * cp]], tag="pvA")
            c.op("tensor", lambda e, k=k1, h_=hp, d=pslotP[sl][0], b=base: e.matmul(
                ppv[0:65, b + 512:b + 1024],
                vA[:, k * 1040 + (2 * h_ + 1) * 65: k * 1040 + (2 * h_ + 1) * 65 + 65],
                d[:, 512:1024], start=(k == 0), stop=(k == NT - 1)),
                ins=[pslotP[sl][1], vA_b[k1]], outs=[ppvB[2 * cp + 1]], tag="pvB")

        den, bden = wkB[3]
        bc_sb, bbc = wkB[2]
        dden_b = V("dden")

        def emit_ep1(hp, qf4, cp):
            b = cp * 1024
            c.op("vector", lambda e, b_=b: e.reciprocal(
                den[64:65, 0:512], ppv[64:65, b_:b_ + 512]),
                ins=[ppvB[2 * cp]], outs=[bden], tag="recip")
            c.op("vector", lambda e, b_=b: e.reciprocal(
                den[64:65, 512:1024], ppv[64:65, b_ + 512:b_ + 1024]),
                ins=[ppvB[2 * cp + 1]], outs=[bden], tag="recip")
            dma("sync", dden, den[64:65, 0:1024], ins=[bden], outs=[dden_b],
                tag="stden", sem="q_den")

        def emit_ep2(hp, qf4, cp):
            b = cp * 1024
            exo = c.fence(qT_b[hp][qf4 // 2])
            dma("sync", bc_sb[0:64, 0:512], dden[0:1, 0:512].broadcast_to((64, 512)),
                ins=[dden_b], outs=[bbc], tag="ldbca", sem="q_bca")
            c.op("vector", lambda e, h_=hp, q4=qf4, b_=b: e.tensor_mul(
                oT[0:64, h_ * N + q4 * 512:h_ * N + q4 * 512 + 512],
                ppv[0:64, b_:b_ + 512], bc_sb[0:64, 0:512]),
                ins=[ppvB[2 * cp], bbc], outs=[oT_b[2 * hp][qf4 // 2]], extra=exo, tag="onormA")
            dma("sync", bc_sb[0:64, 512:1024], dden[0:1, 512:1024].broadcast_to((64, 512)),
                ins=[dden_b], outs=[bbc], tag="ldbcb", sem="q_bcb")

        def emit_ep3(hp, qf4, cp):
            b = cp * 1024
            c.op("vector", lambda e, b_=b: e.tensor_mul(
                btm[0:64, 0:512], ppv[0:64, b_ + 512:b_ + 1024], bc_sb[0:64, 512:1024]),
                ins=[ppvB[2 * cp + 1], bbc], outs=[bbtm], tag="onormBm")
            dma("sync", oT[64:128, hp * N + qf4 * 512:hp * N + qf4 * 512 + 512],
                btm[0:64, 0:512], ins=[bbtm], outs=[oT_b[2 * hp + 1][qf4 // 2]],
                extra=c.fence(qT_b[hp][qf4 // 2]), tag="shB", sem="q_shb")

        chunks = [(hp, qf4) for hp in range(8) for qf4 in range(4)]
        prev = None
        for ci, (hp, qf4) in enumerate(chunks):
            cp = ci % 2
            for kc in range(NT):
                emit_S(hp, qf4, kc, first=(ci == 0 and kc == 0))
                if prev is not None:
                    if kc == 1:
                        emit_ep1(*prev)
                    elif kc == 2:
                        emit_ep2(*prev)
                    elif kc == 3:
                        emit_ep3(*prev)
                if kc >= 3:
                    emit_pv(hp, kc - 3, cp)
                emit_exp(kc)
            for k1 in (NT - 3, NT - 2, NT - 1):
                emit_pv(hp, k1, cp)
            prev = (hp, qf4, cp)
        emit_ep1(*prev)
        emit_ep2(*prev)
        emit_ep3(*prev)

        _maybe_stop("S2")
        wsh_b = [V("wsh0"), V("wsh1")]
        fence_attnspan = c.fence(*(oT_b[h][qf] for h in range(H) for qf in range(2)))
        fence_wo = c.fence(*(V(f"wo_{ic}") for ic in range(8)))
        for ik in range(11):
            dma("sync", ws2_b[:, ik * D:(ik + 1) * D], ws2b[ik * 128:(ik + 1) * 128, :],
                outs=[V(f"ws2_{ik}")],
                extra=list(fence_attnspan) + (list(c.fence(pcsh_b)) if ik == 0 else []), tag="ldws2")
        c.finalize_group("q_ldws2", *(V(f"ws2_{ik}") for ik in range(11)))
        ytile_b = [V(f"yt_{it}") for it in range(NT)]
        for dc in range(8):
            for tf in range(2):
                pb = pmB[(dc * 2 + tf) % 2]
                pa = pmA[(dc * 2 + tf) % 2]
                for h2 in range(2):
                    for ic in range(8):
                        c.op("tensor", lambda e, i=ic, d=dc, f=tf, hh2=h2, p=pa: e.matmul(
                            p[:, hh2 * 512:hh2 * 512 + 512],
                            wo_b[:, i * D + d * 128: i * D + d * 128 + 128],
                            oT[:, i * N + f * 1024 + hh2 * 512: i * N + f * 1024 + hh2 * 512 + 512],
                            start=(i == 0), stop=(i == 7)),
                            ins=[V(f"wo_{ic}"), oT_b[2 * ic][tf], oT_b[2 * ic + 1][tf]],
                            outs=[pb], tag="proj",
                            extra=(c.fence(pm1hB[0], pm1hB[1], pm2hB[0], pm2hB[1])
                                   if dc == 0 and tf == 0 and h2 == 0 and ic == 0 else ()))
                pj, bpj = wkB[tf % 2]
                c.op("scalar", lambda e, p=pa, d=pj, dcc=dc: e.activation(
                    d, p[:, 0:1024], AF.Identity, bias=boutc[:, dcc:dcc + 1]),
                    ins=[pb, V("boutc")], outs=[bpj], tag="projevac")
                for tt in range(8):
                    it = tf * 8 + tt
                    sl = tt % 2
                    c.op("tensor", lambda e, s=sl, d=pj, t=tt: e.transpose(
                        pslot(s), d[:, t * 128:(t + 1) * 128], identf),
                        ins=[bpj, V("identf")], outs=[ppvB[sl]], tag="tppj")
                    c.op("vector", lambda e, s=sl, i=it, dcc=dc: e.tensor_copy(
                        yacc[:, i * D + dcc * 128: i * D + dcc * 128 + 128], pslot(s)),
                        ins=[ppvB[sl]], outs=([ytile_b[it]] if dc == 7 else []),
                        extra=(c.fence(ytile_b[it]) if dc == 7 else ()), tag="pjst")

        fence_oTread = c.fence(*(oT_b[h][qf] for h in range(H) for qf in range(2)))
        for sj0 in range(2):
            srcap0 = ws13b[:, sj0 * 256:(sj0 + 1) * 256].rearrange("(k p) cc -> p k cc", p=128)
            dma("sync", wshdb[:, sj0 * 2048:(sj0 + 1) * 2048], srcap0,
                outs=[wsh_b[sj0]], extra=fence_oTread, tag="ldwsh", sem="q_wsh%d" % sj0)

        amat_b = [V(f"amat_{it}") for it in range(NT)]
        ridxf_b = [V(f"ridxf_{it}") for it in range(NT)]
        rwk_b = [V(f"rwk_{it}") for it in range(NT)]
        sinv_b = [V(f"sinv_{it}") for it in range(NT)]
        gcof_b = V("gcof")

        wk1 = wk[:, 1024:2048]

        def emit_r1_tile(it):
            par = it % 2
            rb = ppvB[2 + par]
            pv64 = ppv[:, 1024 + par * 512:1024 + par * 512 + 64]
            lg, blg = wk1[:, par * 64:par * 64 + 64], V(f"r1lg{par}")
            bi, bbi = wk1[:, 128 + par * 64:128 + par * 64 + 64], V(f"r1bi{par}")
            oh, boh = wk1[:, 256 + par * 64:256 + par * 64 + 64], V(f"r1oh{par}")
            wtmp, bwt = wk1[:, 384 + par * 64:384 + par * 64 + 64], V(f"r1wt{par}")
            nm_, bnm = stat[:, 38 + par:39 + par], V(f"r1nm{par}")
            sm, bsm = stat[:, 40 + par:41 + par], V(f"r1sm{par}")
            sv, bsv = stat[:, 42 + par:43 + par], V(f"r1sv{par}")
            rtp, brtp = rtop[:, par * 8:par * 8 + 8], V(f"rtop{par}")
            rix, brix = ridx[:, par * 8:par * 8 + 8], V(f"ridx{par}")
            for ic in range(8):
                c.op("tensor", lambda e, i=ic, t=it, p=pv64: e.matmul(
                    p, x1T[:, i * N + t * 128: i * N + t * 128 + 128],
                    gw_b[:, i * E:(i + 1) * E], start=(i == 0), stop=(i == 7)),
                    ins=[V(f"gw_{ic}"), x1T_b[ic][it]], outs=[rb], tag="gate")
            c.op("vector", lambda e, p=pv64, s=nm_: e.reduce_max(s, p, axis=AX.X, negate=True),
                 ins=[rb], outs=[bnm], tag="gmax")
            c.op("scalar", lambda e, p=pv64, s=nm_, a=sm, d=lg: e.activation(
                d, p, AF.Exp, bias=s, accum_out=a),
                ins=[rb, bnm], outs=[blg, bsm],
                extra=(c.fence(wkB[1][1]) if it == 0 else ()), tag="gexp")
            c.op("vector", lambda e, a=sm, s=sv: e.reciprocal(s, a), ins=[bsm], outs=[bsv])
            c.op("vector", lambda e, s=sv, t=it: e.tensor_copy(sinv_ap(t), s),
                 ins=[bsv], outs=[sinv_b[it]])
            c.op("vector", lambda e, s=sv, d=bi, l_=lg: e.scalar_tensor_tensor(
                d, l_, s, gateb_b,
                op0=ALU.mult, op1=ALU.add),
                ins=[blg, bsv, V("gateb_b")], outs=[bbi],
                extra=(c.fence(wkB[1][1]) if it == 0 else ()), tag="bias")
            c.op("vector", lambda e, d=bi, r=rtp: e.max(out=r, in_=d),
                 ins=[bbi], outs=[brtp], tag="top8")
            c.op("vector", lambda e, d=bi, r=rtp, x=rix: e.max_index(out=x, in_max=r, in_values=d),
                 ins=[brtp, bbi], outs=[brix], tag="idx8")
            c.op("vector", lambda e, t=it, x=rix: e.tensor_copy(ridxf[:, t * 8:t * 8 + 8], x),
                 ins=[brix], outs=[ridxf_b[it]], tag="idxf")
            for k in range(TOPK):
                c.op("vector", lambda e, t=it, kk=k, d=oh: e.tensor_scalar(
                    d, iota64, ridxf[:, t * 8 + kk:t * 8 + kk + 1], None, op0=ALU.is_equal),
                    ins=[ridxf_b[it], V("iota64")], outs=[boh], tag="oneh")
                if k == 0:
                    c.op("vector", lambda e, t=it, d=oh: e.tensor_copy(
                        amat[:, t * E:(t + 1) * E], d),
                        ins=[boh], outs=[amat_b[it]])
                else:
                    c.op("vector", lambda e, t=it, d=oh: e.tensor_add(
                        amat[:, t * E:(t + 1) * E], amat[:, t * E:(t + 1) * E], d),
                        ins=[boh], outs=[amat_b[it]])
                c.op("vector", lambda e, d=oh, w=wtmp, l_=lg: e.tensor_mul(
                    w, d, l_),
                    ins=[boh, blg], outs=[bwt])
                c.op("vector", lambda e, t=it, kk=k, w=wtmp: e.reduce_sum(
                    rwk[:, t * 8 + kk:t * 8 + kk + 1], w, axis=AX.X),
                    ins=[bwt], outs=[rwk_b[it]], tag="wk")
            c.op("tensor", lambda e, t=it, p=pv64: e.matmul(
                p[0:1, 0:64], onesc[:, 0:1], amat[:, t * E:(t + 1) * E], start=True, stop=True),
                ins=[amat_b[it], V("onesc")], outs=[rb], tag="tot")
            c.op("vector", lambda e, t=it, p=pv64: e.tensor_copy(gcof[0:1, t * E:(t + 1) * E], p[0:1, :]),
                 ins=[rb], outs=[gcof_b], tag="totcp")

        x1T_b = [[V(f"x1T_{kc}_{it}") for it in range(NT)] for kc in range(8)]
        x1m_b = V("x1_mine")

        dump = wk[:, 3 * 1024:4 * 1024]
        bdump = V("lndump")

        def plain_ln(ya, yab, tag, par=0, first=False):
            sm, bsm = stat[:, 32 + par:33 + par], V(f"lnsm{par}")
            c.op("vector", lambda e, y=ya, s=sm: e.reduce_sum(s, y, axis=AX.X),
                 ins=[yab], outs=[bsm])
            c.op("vector", lambda e, s=sm: e.tensor_scalar_mul(s, s, 1.0 / D),
                 ins=[bsm], outs=[bsm])
            c.op("vector", lambda e, y=ya, s=sm: e.tensor_scalar(y, y, s, None, op0=ALU.subtract),
                 ins=[bsm], outs=[yab])
            sq, bsq = stat[:, 34 + par:35 + par], V(f"lnsq{par}")
            c.op("scalar", lambda e, y=ya, du=dump, s=sq: e.activation(
                du, y, AF.Square, accum_out=s), ins=[yab], outs=[bdump, bsq],
                extra=(c.fence(wkB[3][1]) if first else ()))
            sr, bsr = stat[:, 36 + par:37 + par], V(f"lnsr{par}")
            c.op("scalar", lambda e, s=sq, r=sr: e.activation(r, s, AF.Sqrt, bias=epsc[:, 0:1], scale=1.0 / D),
                 ins=[bsq, V("epsc")], outs=[bsr])
            c.op("vector", lambda e, r=sr: e.reciprocal(r, r), ins=[bsr], outs=[bsr])
            c.op("vector", lambda e, y=ya, r=sr: e.tensor_scalar(y, y, r, None, op0=ALU.mult),
                 ins=[bsr], outs=[yab], tag=tag)

        for it in range(NT):
            par = it % 2
            xa = wk[:, 2 * 1024:3 * 1024] if par == 0 else wk[:, 0:1024]
            bxa = V(f"xar{par}")
            dma("scalar", xa, x_in[it * 128:(it + 1) * 128, :], outs=[bxa],
                extra=(c.fence(wkB[2 if par == 0 else 0][1]) if it < 2 else ()), tag="ldx2")
            ya = yacc[:, it * D:(it + 1) * D]
            yab = ytile_b[it]
            c.op("vector", lambda e, a=xa, y=ya: e.tensor_add(y, y, a),
                 ins=[bxa], outs=[yab], tag="resid")
            plain_ln(ya, yab, "ln1", par=par, first=(it == 0))
            for dcj in range(8):
                sl = dcj % 2
                c.op("tensor", lambda e, d=dcj, s=sl, y=ya: e.transpose(
                    pslot(s), y[:, d * 128:(d + 1) * 128], identf),
                    ins=[yab, V("identf")], outs=[ppvB[sl]], tag="tpx1")
                c.op("vector", lambda e, d=dcj, s=sl, i=it: e.tensor_copy(
                    x1T[:, d * N + i * 128: d * N + i * 128 + 128], pslot(s)),
                    ins=[ppvB[sl]], outs=[x1T_b[dcj][it]])
            dma("gpsimd", x1_mine[it * 128:(it + 1) * 128, :], ya,
                ins=[yab], outs=[x1m_b], tag="x1st")
            if tap == "x1":
                dma("scalar", taps["t_x1"][it * 128:(it + 1) * 128, :], ya,
                    ins=[yab], outs=[V("tapx1")], tag="tapx1")

        x1a_b = V("x1_all")

        cntall_b = V("cnt_all")

        def emit_collectives():
            dma("gpsimd", cnt_mine.rearrange("t e -> (t e)").rearrange("(one f) -> one f", one=1),
                gcof[0:1, :], ins=[gcof_b], outs=[V("cnt_mine")], tag="cntst")
            c.op("gpsimd", lambda e: e.collective_compute(
                "AllGather", ALU.bypass, replica_groups=[list(range(NC_))],
                ins=[cnt_mine], outs=[cnt_all]),
                ins=[V("cnt_mine")], outs=[cntall_b], inc=1, tag="agcnt")
            c.op("gpsimd", lambda e: e.collective_compute(
                "AllGather", ALU.bypass, replica_groups=[list(range(NC_))],
                ins=[x1_mine], outs=[x1_all]),
                ins=[x1m_b], outs=[x1a_b], inc=1, tag="agx1")

        def emit_r2():
            dma("sync", scnt, cnt_all, ins=[cntall_b], outs=[V("scnt")], tag="ldcnt")
            c.op("tensor", lambda e: e.matmul(ppv[:, 1024:1088], trix, scnt, start=True, stop=True),
                 ins=[V("scnt"), V("trix")], outs=[ppvB[2]], tag="gcum")
            c.op("vector", lambda e: e.tensor_copy(sgco, ppv[:, 1024:1088]), ins=[ppvB[2]], outs=[V("sgco")])
            c.op("tensor", lambda e: e.matmul(ppv[0:16, 1536:1600], sel, sgco, start=True, stop=True),
                 ins=[V("sgco"), V("sel")], outs=[ppvB[3]], tag="selrows")
            myco, bmyco = wkB[3]
            c.op("vector", lambda e: e.tensor_copy(myco[0:16, 0:64], ppv[0:16, 1536:1600]),
                 ins=[ppvB[3]], outs=[bmyco])
            dma("sync", gco_dram, myco[0:16, 0:64], ins=[bmyco], outs=[V("gco_dram")], tag="gcodma1")
            dma("sync", gcof[0:1, :], gco_dram.rearrange("p f -> (p f)").rearrange("(one f) -> one f", one=1),
                ins=[V("gco_dram")], outs=[gcof_b], tag="gcodma2")

        raddr_b = [V(f"raddr_{it}") for it in range(NT)]
        last_scat = [None]

        def emit_r3_tile(it):
            par = it % 2
            rb = ppvB[2 + par]
            pv64 = ppv[:, 1024 + par * 512:1024 + par * 512 + 64]
            c.op("tensor", lambda e, t=it, p=pv64: e.matmul(
                p, trix, amat[:, t * E:(t + 1) * E], start=True, stop=False),
                ins=[amat_b[it], V("trix")], outs=[rb], tag="cpos1")
            c.op("tensor", lambda e, t=it, p=pv64: e.matmul(
                p, onesr[0:1, :], gcof[0:1, t * E:(t + 1) * E], start=False, stop=True),
                ins=[gcof_b, V("onesr")], outs=[rb], tag="cpos2")
            oh, boh = wk1[:, 512 + par * 64:512 + par * 64 + 64], V(f"r3oh{par}")
            ps_, bps = wk1[:, 640 + par * 128:640 + par * 128 + 128], V(f"r3ps{par}")
            cpv, bcpv = wk[:, par * 64:par * 64 + 64], V(f"r3cp{par}")
            c.op("vector", lambda e, p=pv64, d=cpv: e.tensor_copy(d, p),
                 ins=[rb], outs=[bcpv], tag="cposcp")
            for k in range(TOPK):
                c.op("vector", lambda e, t=it, kk=k, d=oh: e.tensor_scalar(
                    d, iota64, ridxf[:, t * 8 + kk:t * 8 + kk + 1], None, op0=ALU.is_equal),
                    ins=[ridxf_b[it], V("iota64")], outs=[boh], tag="oneh2")
                c.op("vector", lambda e, d=oh, p=cpv, w=ps_: e.tensor_mul(
                    w[:, 0:64], d, p),
                    ins=[boh, bcpv], outs=[bps])
                c.op("vector", lambda e, kk=k, w=ps_: e.reduce_sum(
                    w[:, 64 + kk:65 + kk], w[:, 0:64], axis=AX.X),
                    ins=[bps], outs=[bps], tag="posk")
            c.op("vector", lambda e, w=ps_: e.tensor_scalar(
                w[:, 72:78], w[:, 64:70], 512.0, None, op0=ALU.is_lt),
                ins=[bps], outs=[bps], tag="keep")
            c.op("vector", lambda e, t=it, w=ps_: e.tensor_mul(
                rwk[:, t * 8:t * 8 + 6], rwk[:, t * 8:t * 8 + 6], w[:, 72:78]),
                ins=[bps], outs=[rwk_b[it]], tag="wmask")
            c.op("vector", lambda e, t=it: e.tensor_scalar(
                rwk[:, t * 8:t * 8 + 6], rwk[:, t * 8:t * 8 + 6], sinv_ap(t), None, op0=ALU.mult),
                ins=[sinv_b[it]], outs=[rwk_b[it]], tag="wnorm")
            c.op("vector", lambda e, t=it, w=ps_: e.tensor_scalar(
                w[:, 80:86], ridxf[:, t * 8:t * 8 + 6], 512.0, None, op0=ALU.mult),
                ins=[ridxf_b[it]], outs=[bps], tag="addr1")
            c.op("vector", lambda e, w=ps_: e.tensor_add(
                w[:, 80:86], w[:, 80:86], w[:, 64:70]),
                ins=[bps], outs=[bps], tag="addr2")
            c.op("vector", lambda e, w=ps_: e.tensor_scalar(
                w[:, 88:94], w[:, 72:78], -1e6, 1e6, op0=ALU.mult, op1=ALU.add),
                ins=[bps], outs=[bps], tag="addr3")
            c.op("vector", lambda e, w=ps_: e.tensor_add(
                w[:, 80:86], w[:, 80:86], w[:, 88:94]),
                ins=[bps], outs=[bps], tag="addr4")
            c.op("vector", lambda e, t=it, w=ps_: e.tensor_copy(
                raddr[:, t * 8:t * 8 + 6], w[:, 80:86]),
                ins=[bps], outs=[raddr_b[it]], tag="addru")
            oh2, boh2 = oh, boh
            ge, bge = wk1[:, 896 + par * 64:896 + par * 64 + 64], V(f"r3ge{par}")
            for k in range(TOPK):
                c.op("vector", lambda e, t=it, kk=k, d=oh2: e.tensor_scalar(
                    d, iota64, ridxf[:, t * 8 + kk:t * 8 + kk + 1], None, op0=ALU.is_equal),
                    ins=[ridxf_b[it], V("iota64")], outs=[boh2], tag="ga_oneh")
                c.op("vector", lambda e, d=oh2, g=ge: e.tensor_mul(
                    g, d, gbase),
                    ins=[boh2, V("gbase")], outs=[bge], tag="ga_mul")
                c.op("vector", lambda e, kk=k, g=ge, w=ps_: e.reduce_sum(
                    w[:, 96 + kk:97 + kk], g, axis=AX.X),
                    ins=[bge], outs=[bps], tag="ga_red")
            c.op("vector", lambda e, w=ps_: e.tensor_add(
                w[:, 96:102], w[:, 96:102], w[:, 64:70]),
                ins=[bps], outs=[bps], tag="ga_s3")
            c.op("vector", lambda e, w=ps_: e.tensor_add(
                w[:, 96:102], w[:, 96:102], w[:, 88:94]),
                ins=[bps], outs=[bps], tag="ga_oob")
            c.op("vector", lambda e, t=it, w=ps_: e.tensor_copy(
                raddr2[:, t * 8:t * 8 + 6], w[:, 96:102]),
                ins=[bps], outs=[V(f"raddr2_{it}")], tag="ga_u")
            for k in range(TOPK):
                last_scat[0] = c.op("gpsimd", lambda e, t=it, kk=k: e.indirect_dma_start(
                    out=slot_part,
                    out_offset=IndirectOffsetOnAxis(raddr[:, t * 8 + kk:t * 8 + kk + 1], 0),
                    in_=tokid[:, t:t + 1], in_offset=None,
                    bounds_check=bcr(e, E * CAP - 1), oob_is_err=False),
                    ins=[raddr_b[it], V("tokid")], outs=[],
                    extra=list(spart_b.w), inc=16, tag="scat", sem="q_scat")

        gsh_b = [V(f"gsh_{j}") for j in range(11)]
        r1_done = [0]
        for tq in range(2):
            for sj in range(11):
                slot = sj % 2
                if not (tq == 0 and sj < 2):
                    srcap = ws13b[:, sj * 256:(sj + 1) * 256].rearrange("(k p) cc -> p k cc", p=128)
                    dma("sync", wshdb[:, slot * 2048:(slot + 1) * 2048], srcap,
                        outs=[wsh_b[slot]], extra=fence_attnspan, tag="ldwsh", sem="q_wsh%d" % slot)
                for half in range(2):
                    pa = pmA[half]
                    pb = pmB[half]
                    for h2 in range(2):
                        for kc in range(8):
                            c.op("tensor", lambda e, k=kc, s=slot, q=tq, hf=half, hh2=h2, p=pa: e.matmul(
                                p[:, hh2 * 512:hh2 * 512 + 512],
                                wshdb[:, s * 2048 + k * 256 + hf * 128: s * 2048 + k * 256 + hf * 128 + 128],
                                x1T[:, k * N + q * 1024 + hh2 * 512: k * N + q * 1024 + hh2 * 512 + 512],
                                start=(k == 0), stop=(k == 7)),
                                ins=[wsh_b[slot]] + [x1T_b[kc][tq * 8 + itt] for itt in range(8)],
                                outs=[pb], tag="sh13")
                c.op("scalar", lambda e: e.activation(shsilu[:, 0:1024], pm1[:, 0:1024], AF.Silu),
                     ins=[pmB[0]], outs=[V("shsilu")], tag="shsilu")
                c.op("vector", lambda e, j=sj, q=tq: e.tensor_mul(
                    gsh[:, j * 1024:(j + 1) * 1024], shsilu[:, 0:1024], pm2[:, 0:1024]),
                    ins=[V("shsilu"), pmB[1]], outs=[gsh_b[sj]], tag="shmul")
                if tq == 0:
                    while r1_done[0] < (sj + 1) * NT // 11:
                        emit_r1_tile(r1_done[0])
                        r1_done[0] += 1
            if tq == 0:
                emit_collectives()
                emit_r2()
            for mt in range(8):
                m = tq * 8 + mt
                pb = pmB[mt % 2]
                pa = pmA[mt % 2]
                for h2 in range(2):
                    for ik in range(11):
                        c.op("tensor", lambda e, i=ik, mm=mt, q=tq, hh2=h2, p=pa: e.matmul(
                            p[:, hh2 * 512:hh2 * 512 + 512],
                            gsh[:, i * 1024 + mm * 128: i * 1024 + mm * 128 + 128],
                            ws2_b[:, i * D + hh2 * 512: i * D + hh2 * 512 + 512],
                            start=(i == 0), stop=(i == 10)),
                            ins=[V(f"ws2_{ik}"), gsh_b[ik]], outs=[pb], tag="zmm")
                ya = yacc[:, m * D:(m + 1) * D]
                c.op("vector", lambda e, p=pa, y=ya: e.tensor_add(y, y, p[:, 0:1024]),
                     ins=[pb], outs=[ytile_b[m]], tag="zadd")
                if tq == 1:
                    emit_r3_tile(2 * mt)
                    emit_r3_tile(2 * mt + 1)

        _maybe_stop("SH")
        if tap == "route":
            dma("scalar", taps["t_rwk"], rwk, ins=list(rwk_b), outs=[V("taprwk")], tag="tapr")
            dma("scalar", taps["t_raddr"], raddr, ins=list(raddr_b), outs=[V("tapraddr")], tag="tapr")
            dma("scalar", taps["t_ridxf"], ridxf, ins=list(ridxf_b), outs=[V("tapridxf")], tag="tapr")

        slota_b = V("slot_all")
        c.op("gpsimd", lambda e: e.collective_compute(
            "AllReduce", ALU.add, replica_groups=[list(range(NC_))],
            ins=[slot_part], outs=[slot_all]),
            ins=[spart_b], outs=[slota_b], extra=[last_scat[0]], inc=1, tag="arslot")
        if tap == "route":
            dma("scalar", taps["t_slot"], slot_all, ins=[slota_b], outs=[V("tapslot")], tag="tapr")

        _maybe_stop("R")
        fence_x1T = c.fence(*(x1T_b[kc][it] for kc in range(8) for it in range(NT)))
        fence_gshspan = c.fence(*gsh_b, wsh_b[0], wsh_b[1])
        c.op("gpsimd", lambda e: e.indirect_dma_start(
            out=slotsf[0:32, 0:128], out_offset=None,
            in_=slot_all.rearrange("(r f) one -> r (f one)", f=128),
            in_offset=IndirectOffsetOnAxis(slotrow[0:32, 0:1], 0),
            bounds_check=bcr(e, E * CAP // 128 - 1), oob_is_err=False),
            ins=[slota_b, V("slotrow")], outs=[V("slotsf")], inc=16, tag="ldslot", sem="q_ldslot")
        c.op("tensor", lambda e: e.transpose(ppv[:, 0:32], slotsf[0:32, 0:128], identf[0:32, 0:32]),
             ins=[V("slotsf"), V("identf")], outs=[ppvB[0]], tag="tpslot")
        c.op("vector", lambda e: e.tensor_copy(sidxf, ppv[:, 0:32]),
             ins=[ppvB[0]], outs=[V("sidxf")])
        msk, bmsk = wkB[0]
        c.op("vector", lambda e, m=msk: e.tensor_scalar(m[:, 0:32], sidxf, 0.0, 1e6,
                                                        op0=ALU.is_equal, op1=ALU.mult),
             ins=[V("sidxf")], outs=[bmsk],
             extra=c.fence(V("r3cp0"), V("r3cp1")))
        c.op("vector", lambda e: e.tensor_scalar_add(sidxf, sidxf, -2.0),
             ins=[], outs=[V("sidxf")], extra=c.fence(V("sidxf"), bmsk), tag="sidxm2")
        c.op("vector", lambda e, m=msk: e.tensor_add(sidxf, sidxf, m[:, 0:32]),
             ins=[bmsk], outs=[V("sidxf")])
        c.op("vector", lambda e: e.tensor_copy(sidx, sidxf),
             ins=[], outs=[V("sidx")], extra=c.fence(V("sidxf")), tag="sidxu")
        if tap == "expert":
            dma("scalar", taps["t_sidx"], sidxf, ins=[V("sidxf")], outs=[V("tapsidx")], tag="tape")

        ew13_s = [V("ew13s0"), V("ew13s1")]
        ew2_s = [V("ew2s0"), V("ew2s1")]
        hT_b = V("hT")
        xbTa = [xbT, xbT1, xbT2, xbT3, xbT4]
        xbT_b = [V(f"xbTs{sl}") for sl in range(5)]
        c.op("vector", lambda e: e.memset(hT[64:128, 5 * 512:6 * 512], 0.0),
             outs=[V("hTz")], extra=fence_gshspan, tag="hTz")
        c.op("vector", lambda e: e.memset(xbt[:, 0:1024], 0.0), outs=[V("xbt0")],
             extra=fence_x1T + fence_gshspan, tag="xbtz")
        c.op("vector", lambda e: e.memset(xbt[:, 1024:2048], 0.0), outs=[V("xbt1")],
             extra=fence_x1T + fence_gshspan, tag="xbtz")
        ybm_b2 = [V("yb_mine0"), V("yb_mine1"), V("yb_mine2")]
        ych_start = {0: 0, 1: 0, 2: 0, 3: 1, 4: 1, 5: 1, 6: 2, 7: 2}
        ybs_b = [V("ybs0"), V("ybs1")]
        xbt_bufs = [V("xbt0"), V("xbt1")]

        def emit_gather(el):
            sl5 = el % 5
            for sc2 in range(4):
                xsl = (el * 4 + sc2) % 2
                c.op("gpsimd", lambda e, l=el, s=sc2, x=xsl: e.indirect_dma_start(
                    out=xbt[:, x * 1024:x * 1024 + 1024], out_offset=None,
                    in_=x1_all, in_offset=IndirectOffsetOnAxis(sidx[:, l * 4 + s:l * 4 + s + 1], 0),
                    bounds_check=bcr(e, B * N - 1), oob_is_err=False),
                    ins=[x1a_b, V("sidx")], outs=[xbt_bufs[xsl]], inc=16, tag="gxb",
                    sem="q_gxb%d" % xsl,
                    extra=(fence_gshspan if el == 0 and sc2 < 2 else ()))
                c.op("scalar", lambda e, x=xsl: e.activation(
                    xbtf[:, 0:1024], xbt[:, x * 1024:x * 1024 + 1024], AF.Copy),
                    ins=[xbt_bufs[xsl]], outs=[V("xbtf")], tag="xbcast")
                for dcj in range(8):
                    sl = dcj % 4
                    c.op("tensor", lambda e, d=dcj, s=sl: e.transpose(
                        pslot(s), xbtf[:, d * 128:(d + 1) * 128], identf),
                        ins=[V("xbtf"), V("identf")], outs=[ppvB[sl]], tag="tpxb")
                    c.op("vector", lambda e, d=dcj, s=sl, ss=sc2, t5=sl5: e.tensor_copy(
                        xbTa[t5][:, d * CAP + ss * 128: d * CAP + ss * 128 + 128], pslot(s)),
                        ins=[ppvB[sl]], outs=[xbT_b[sl5]],
                        extra=(fence_gshspan if el == 0 and sc2 == 0 else ()), tag="xbTst")

        for el0 in range(5):
            emit_gather(el0)
        def emit_ldew2(el):
            esl = el % 2
            dma("scalar", ew2_db[:, esl * 6 * D:(esl + 1) * 6 * D],
                ew2b[el].rearrange("(k p) dd -> p k dd", p=128),
                outs=[ew2_s[esl]],
                extra=(fence_x1T + c.fence(V("pcast")) if el == 0 else ()), tag="ldew2",
                sem="q_ew2s%d" % esl)

        def emit_ldew13(el, ij):
            w = 128 if ij < 5 else 64
            jp = ij % 2
            dma("scalar", ew13_db[:, jp * 2048:jp * 2048 + 2 * w * 8],
                ew13b[el][:, ij * 256:ij * 256 + 2 * w].rearrange("(k p) cc -> p k cc", p=128),
                outs=[ew13_s[jp]],
                extra=(fence_x1T + c.fence(V("pcast")) if el == 0 and ij < 2 else ()),
                tag="ldew13", sem="q_e13s%d" % jp)

        emit_ldew2(0)
        emit_ldew13(0, 0)
        emit_ldew13(0, 1)
        for el in range(EPC):
            esl = el % 2
            sl5 = el % 5
            for ij in range(6):
                w = 128 if ij < 5 else 64
                jp = ij % 2
                for half in range(2):
                    pa = pm1 if half == 0 else pm2
                    pb = (pm1hB if half == 0 else pm2hB)[jp]
                    for kc in range(8):
                        c.op("tensor", lambda e, k=kc, j=ij, hf=half, w_=w, p=pa, t5=sl5: e.matmul(
                            p[0:w_, (j % 2) * 512:(j % 2) * 512 + 512],
                            ew13_db[:, (j % 2) * 2048 + k * 2 * w_ + hf * w_:
                                    (j % 2) * 2048 + k * 2 * w_ + hf * w_ + w_],
                            xbTa[t5][:, k * CAP: k * CAP + 512],
                            start=(k == 0), stop=(k == 7)),
                            ins=[ew13_s[jp], xbT_b[sl5]], outs=[pb],
                            extra=(c.fence(pmB[0], pmB[1]) if el == 0 and ij == 0 and half == 0 and kc == 0 else ()),
                            tag="e13")
                if ij < 4:
                    emit_ldew13(el, ij + 2)
                elif ij == 4:
                    if el + 1 < EPC:
                        emit_ldew2(el + 1)
                else:
                    if el + 1 < EPC:
                        emit_ldew13(el + 1, 0)
                        emit_ldew13(el + 1, 1)
                c.op("scalar", lambda e, w_=w, j=jp: e.activation(
                    shsilu[0:w_, 0:512], pm1[0:w_, j * 512:j * 512 + 512], AF.Silu),
                    ins=[pm1hB[jp]], outs=[V("shsilu")], tag="esilu")
                c.op("vector", lambda e, j=ij, w_=w, jp_=jp: e.tensor_mul(
                    hT[0:w_, j * 512:(j + 1) * 512], shsilu[0:w_, 0:512], pm2[0:w_, jp_ * 512:jp_ * 512 + 512]),
                    ins=[V("shsilu"), pm2hB[jp]], outs=[hT_b], tag="emul")
            if el + 5 < EPC:
                emit_gather(el + 5)
            for sc2 in range(4):
                pa = pm1 if sc2 % 2 == 0 else pm2
                phB = pm1hB if sc2 % 2 == 0 else pm2hB
                for h2 in range(2):
                    for ik in range(6):
                        c.op("tensor", lambda e, i=ik, s=sc2, sl_=esl, hh2=h2, p=pa: e.matmul(
                            p[:, hh2 * 512:hh2 * 512 + 512],
                            hT[:, i * 512 + s * 128: i * 512 + s * 128 + 128],
                            ew2_db[:, sl_ * 6 * D + i * D + hh2 * 512: sl_ * 6 * D + i * D + hh2 * 512 + 512],
                            start=(i == 0), stop=(i == 5)),
                            ins=[ew2_s[esl], hT_b, V("hTz")], outs=[phB[h2]], tag="e2")
                bybs = ybs_b[sc2 % 2]
                c.op("vector", lambda e, p=pa, s=sc2: e.tensor_copy(
                    ybs[:, (s % 2) * 1024:(s % 2) * 1024 + 1024], p[:, 0:1024]),
                    ins=[phB[0], phB[1]], outs=[bybs],
                    extra=(c.fence(wsh_b[0], wsh_b[1]) if el == 0 and sc2 < 2 else ()),
                    tag="ybevac")
                dma("sync", yb_mine[el * CAP + sc2 * 128: el * CAP + sc2 * 128 + 128, :],
                    ybs[:, (sc2 % 2) * 1024:(sc2 % 2) * 1024 + 1024],
                    ins=[bybs], outs=[ybm_b2[ych_start[el]]], tag="ybst",
                    sem="q_ybst%d" % ych_start[el])
            if el in (2, 5, 7):
                ce = ych_start[el]
                r0 = (0, 3, 6)[ce] * 512
                r1 = (3, 6, 8)[ce] * 512
                c.op("gpsimd", lambda e, a=r0, b=r1: e.collective_compute(
                    "AllGather", ALU.bypass, replica_groups=[list(range(NC_))],
                    ins=[yb_mine[a:b, :]],
                    outs=[yb_all[a * 8:b * 8, :]]),
                    ins=[ybm_b2[ce]], outs=[V("yb_all")], inc=1, tag="agyb")

        if tap == "expert":
            dma("scalar", taps["t_yb"], yb_mine, ins=list(ybm_b2), outs=[V("tapyb")], tag="tape")
        yba_b = V("yb_all")

        _maybe_stop("EX")
        gb_b = [V(f"gb{g}") for g in range(4)]
        fence_ex = c.fence(*ew2_s, *ew13_s)
        for g in range(4):
            c.op("vector", lambda e, gg=g: e.memset(gbuf[:, gg * 1024:(gg + 1) * 1024], 0.0),
                 outs=[gb_b[g]], extra=fence_ex, tag="gbz")
        out_b = V("out")
        for it in range(NT):
            ya = yacc[:, it * D:(it + 1) * D]
            yab = ytile_b[it]
            for k in range(TOPK):
                g = k % 4
                c.op("gpsimd", lambda e, t=it, kk=k, gg=g: e.indirect_dma_start(
                    out=gbuf[:, gg * 1024:(gg + 1) * 1024], out_offset=None,
                    in_=yb_all, in_offset=IndirectOffsetOnAxis(raddr2[:, t * 8 + kk:t * 8 + kk + 1], 0),
                    bounds_check=bcr(e, E * CAP - 1), oob_is_err=False),
                    ins=[yba_b, V(f"raddr2_{it}")], outs=[gb_b[g]], inc=16,
                    tag="gyb", sem="q_gyb%d" % g)
                c.op("vector", lambda e, t=it, kk=k, gg=g, y=ya: e.scalar_tensor_tensor(
                    y, gbuf[:, gg * 1024:(gg + 1) * 1024], rwk[:, t * 8 + kk:t * 8 + kk + 1], y,
                    op0=ALU.mult, op1=ALU.add),
                    ins=[gb_b[g], rwk_b[it]], outs=[yab], tag="mac")
            plain_ln(ya, yab, "ln2", par=it % 2)
            dma("scalar", out[it * 128:(it + 1) * 128, :], ya, ins=[yab], outs=[out_b], tag="outst")

    except _StopBuild:
        pass
    if not V("out").w:
        dma("scalar", out[0:128, :], wk[:, 0:1024], tag="outst", outs=[V("out")])
    out_b = V("out")
    for t in out_b.w:
        c._wait("scalar", t.sem, t.val)

    from contextlib import ExitStack
    es = ExitStack()
    with es:
        sems = {}
        for e in sorted(set(list(ENGINES) + list(c.cnt.keys()))):
            sems[e] = es.enter_context(nc.semaphore(f"sem_{e}"))
        block = es.enter_context(nc.Block())
        block.sync(lambda p: c.replay("sync", p, sems))
        block.scalar(lambda p: c.replay("scalar", p, sems))
        block.vector(lambda p: c.replay("vector", p, sems))
        block.tensor(lambda p: c.replay("tensor", p, sems))
        block.gpsimd(lambda p: c.replay("gpsimd", p, sems))
    for e in ENGINES:
        assert c.cnt[e] < 60000, (e, c.cnt[e])
    return nc


def host_prep(inputs):
    import ml_dtypes
    x = np.ascontiguousarray(np.asarray(inputs["L0_attn_x"], dtype=np.float32))
    wqkv = np.asarray(inputs["L0_attn_w_qkv"], dtype=np.float32)
    wout = np.ascontiguousarray(np.asarray(inputs["L0_attn_w_out"], dtype=np.float32))
    bout = np.asarray(inputs["L0_attn_b_out"], dtype=np.float32)
    gamma = np.asarray(inputs["L0_attn_ln_gamma"], dtype=np.float32)
    beta = np.asarray(inputs["L0_attn_ln_beta"], dtype=np.float32)
    gate_w = np.asarray(inputs["L0_ffn_gate_w"], dtype=np.float32)
    gate_b = np.asarray(inputs["L0_ffn_gate_b"], dtype=np.float32)
    w1 = np.asarray(inputs["L0_ffn_w1"], dtype=np.float32)
    w2 = np.asarray(inputs["L0_ffn_w2"], dtype=np.float32)
    w3 = np.asarray(inputs["L0_ffn_w3"], dtype=np.float32)
    ws1 = np.asarray(inputs["L0_ffn_ws1"], dtype=np.float32)
    ws2 = np.asarray(inputs["L0_ffn_ws2"], dtype=np.float32)
    ws3 = np.asarray(inputs["L0_ffn_ws3"], dtype=np.float32)

    wqkv_p = np.ascontiguousarray(wqkv)

    boutc = np.ascontiguousarray(bout.reshape(8, 128).T)

    ws13m = np.empty((D, 2 * SI), np.float32)
    w1T, w3T = ws1.T, ws3.T
    for j in range(SI // 128):
        ws13m[:, j * 256:j * 256 + 128] = w1T[:, j * 128:(j + 1) * 128]
        ws13m[:, j * 256 + 128:j * 256 + 256] = w3T[:, j * 128:(j + 1) * 128]
    ws2T = np.ascontiguousarray(ws2.T)

    ew13_all = np.empty((E, D, 2 * I_FF), np.float32)
    w1t = np.transpose(w1, (0, 2, 1))
    w3t = np.transpose(w3, (0, 2, 1))
    for j in range(6):
        w = 128 if j < 5 else 64
        ew13_all[:, :, j * 256:j * 256 + w] = w1t[:, :, j * 128:j * 128 + w]
        ew13_all[:, :, j * 256 + w:j * 256 + 2 * w] = w3t[:, :, j * 128:j * 128 + w]
    ew2T_all = np.zeros((E, 768, D), np.float32)
    ew2T_all[:, :I_FF, :] = np.transpose(w2, (0, 2, 1))

    inv_freq = (1.0 / (10000.0 ** (np.arange(0, HD, 2, dtype=np.float32) / np.float32(HD)))).astype(np.float32)
    pos = np.arange(N, dtype=np.float32)
    ang = pos[:, None] * inv_freq[None, :]
    cs = np.cos(ang).astype(np.float32).T
    sn = np.sin(ang).astype(np.float32).T
    cosf = np.ascontiguousarray(np.tile(cs, (4, 1)))
    sins = np.ascontiguousarray(np.tile(np.concatenate([-sn, sn], 0), (2, 1)))
    permf = np.zeros((128, 128), np.float32)
    for m in range(128):
        permf[(m + 32) % 64 + 64 * (m // 64), m] = 1.0

    shared = dict(
        wqkv=wqkv_p, wout=wout, boutc=boutc,
        gamma=np.ascontiguousarray(gamma.reshape(1, D)),
        beta=np.ascontiguousarray(beta.reshape(1, D)),
        gwT=np.ascontiguousarray(gate_w.T),
        gateb=np.ascontiguousarray(gate_b.reshape(1, E)),
        ws13=ws13m, ws2T=ws2T, cosf=cosf, sins=sins,
        identb=np.eye(128, dtype=np.float32).astype(ml_dtypes.bfloat16),
        permf=permf,
        identf=np.eye(128, dtype=np.float32),
        trix=np.triu(np.ones((128, 128), np.float32), 1),
        onesr=np.ones((1, 128), np.float32),
        onesc=np.ones((128, 1), np.float32),
        iota64=np.tile(np.arange(E, dtype=np.float32), (128, 1)),
        gbase=np.tile(np.array([
            [0, 12288, 24576][min(e % 8 // 3, 2)]
            + (e // 8) * [1536, 1536, 1024][min(e % 8 // 3, 2)]
            + (e % 8 - [0, 3, 6][min(e % 8 // 3, 2)]) * 512
            for e in range(E)], dtype=np.float32), (128, 1)),
    )
    in_maps = []
    p_ar = np.arange(128, dtype=np.float32)
    for cc in range(NC_):
        selm = np.zeros((128, 16), np.float32)
        for j in range(16):
            selm[cc * 16 + j, j] = 1.0
        tokid_m = np.empty((128, NT), np.float32)
        for j in range(NT):
            tokid_m[:, j] = cc * N + j * 128 + p_ar + 2.0
        slotrow_m = (cc * 32 + np.arange(32, dtype=np.uint32)).reshape(32, 1)
        m = dict(shared)
        m["x"] = np.ascontiguousarray(x[cc])
        m["ew13"] = np.ascontiguousarray(ew13_all[cc * EPC:(cc + 1) * EPC])
        m["ew2T"] = np.ascontiguousarray(ew2T_all[cc * EPC:(cc + 1) * EPC])
        m["sel"] = selm
        m["tokid"] = tokid_m
        m["slotrow"] = slotrow_m
        in_maps.append(m)
    return in_maps


_CACHE = {}


def kernel(**inputs):
    in_maps = host_prep(inputs)
    if "nc" not in _CACHE:
        _CACHE["nc"] = build_program()
    nc = _CACHE["nc"]
    res = run_bass_kernel_spmd(nc, in_maps, core_ids=list(range(NC_)))
    return np.stack([res.results[cc]["out"] for cc in range(NC_)], axis=0)


# version 17
# speedup vs baseline: 1.0274x; 1.0274x over previous
import numpy as np
import concourse.bass as bass
import concourse.mybir as mybir
from concourse.bass import IndirectOffsetOnAxis
from concourse.bass_utils import run_bass_kernel_spmd

F32 = mybir.dt.float32
BF16 = mybir.dt.bfloat16
U32 = mybir.dt.uint32
AF = mybir.ActivationFunctionType
ALU = mybir.AluOpType
AX = mybir.AxisListType

B, N, D = 8, 2048, 1024
H, HD = 16, 64
INNER = H * HD
E, TOPK, CAP = 64, 6, 512
I_FF = 704
SI = 1408
NC_ = 8
NT = N // 128
EPC = E // NC_
EXP_C = 4.0
SBUF_BYTES = 212736

ENGINES = ("sync", "scalar", "vector", "tensor", "gpsimd")


class Tok:
    __slots__ = ("sem", "val")

    def __init__(self, sem, val):
        self.sem = sem
        self.val = val


class Buf:
    __slots__ = ("w", "r", "name")

    def __init__(self, name=""):
        self.w = []
        self.r = []
        self.name = name

    def toks(self):
        return self.w + self.r


class Ctx:
    def __init__(self):
        self.rec = {e: [] for e in ENGINES}
        self.cnt = {e: 0 for e in ENGINES}
        self.watermark = {e: {} for e in ENGINES}

    def _wait(self, eng, sem, val):
        if sem == "tensor" and eng == "tensor":
            return
        if self.watermark[eng].get(sem, 0) >= val:
            return
        self.watermark[eng][sem] = val
        self.rec[eng].append(("wait", sem, val))

    def op(self, eng, emit, ins=(), outs=(), extra=(), inc=1, tag="", sem=None):
        best = {}
        for b in ins:
            for t in b.w:
                if best.get(t.sem, 0) < t.val:
                    best[t.sem] = t.val
        for b in outs:
            for t in b.w + b.r:
                if best.get(t.sem, 0) < t.val:
                    best[t.sem] = t.val
        for t in extra:
            if t is not None and best.get(t.sem, 0) < t.val:
                best[t.sem] = t.val
        for s, v in sorted(best.items()):
            self._wait(eng, s, v)
        s_ = sem if sem is not None else eng
        self.cnt[s_] = self.cnt.get(s_, 0) + inc
        tok = Tok(s_, self.cnt[s_])
        self.rec[eng].append(("instr", emit, s_, inc, tag))
        for b in ins:
            b.r.append(tok)
        for b in outs:
            b.w = [tok]
            b.r = []
        return tok

    def finalize_group(self, sem, *bufs):
        tok = Tok(sem, self.cnt.get(sem, 0))
        for b in bufs:
            b.w = [tok]
            b.r = []

    def fence(self, *bufs):
        out = []
        for b in bufs:
            out.extend(b.w)
            out.extend(b.r)
        return out

    def replay(self, eng, proxy, sems):
        for item in self.rec[eng]:
            if item[0] == "wait":
                proxy.wait_ge(sems[item[1]], item[2])
            else:
                _, emit, seng, inc, tag = item
                emit(proxy).then_inc(sems[seng], inc)


def build_program(tap=None, stop_after=None):
    nc = bass.Bass("TRN2", target_bir_lowering=False)

    def ein(name, shape, dt=F32):
        return nc.dram_tensor(name, shape, dt, kind="ExternalInput").ap()

    x_in = ein("x", [N, D])
    wqkv = ein("wqkv", [D, 3 * INNER])
    wout = ein("wout", [INNER, D])
    boutc_i = ein("boutc", [128, 8])
    gamma_i = ein("gamma", [1, D])
    beta_i = ein("beta", [1, D])
    gwT = ein("gwT", [D, E])
    gateb_i = ein("gateb", [1, E])
    ws13 = ein("ws13", [D, 2 * SI])
    ws2T_i = ein("ws2T", [SI, D])
    ew13 = ein("ew13", [EPC, D, 2 * I_FF])
    ew2T = ein("ew2T", [EPC, 768, D])
    cosf_i = ein("cosf", [128, N])
    sins_i = ein("sins", [128, N])
    identb_i = ein("identb", [128, 128], BF16)
    permf_i = ein("permf", [128, 128])
    identf_i = ein("identf", [128, 128])
    trix_i = ein("trix", [128, 128])
    onesr_i = ein("onesr", [1, 128])
    onesc_i = ein("onesc", [128, 1])
    iota64_i = ein("iota64", [128, E])
    gbase_i = ein("gbase", [128, E])
    tokid_i = ein("tokid", [128, NT])
    sel_i = ein("sel", [128, 16])
    slotrow_i = ein("slotrow", [32, 1], U32)

    out = nc.dram_tensor("out", [N, D], F32, kind="ExternalOutput").ap()

    x1_mine = nc.dram_tensor("x1_mine", [N, D], BF16).ap()
    x1_all = nc.dram_tensor("x1_all", [B * N, D], BF16).ap()
    cnt_mine = nc.dram_tensor("cnt_mine", [NT, E], F32).ap()
    cnt_all = nc.dram_tensor("cnt_all", [NC_ * NT, E], F32).ap()
    slot_part = nc.dram_tensor("slot_part", [E * CAP, 1], F32).ap()
    slot_all = nc.dram_tensor("slot_all", [E * CAP, 1], F32).ap()
    yb_mine = nc.dram_tensor("yb_mine", [EPC * CAP, D], BF16).ap()
    ew13b = nc.dram_tensor("ew13b", [EPC, D, 2 * I_FF], BF16).ap()
    ew2b = nc.dram_tensor("ew2b", [EPC, 768, D], BF16).ap()
    ws13b = nc.dram_tensor("ws13b", [1024, 2816], BF16).ap()
    dden = nc.dram_tensor("dden", [1, 1024], F32).ap()
    ws2b = nc.dram_tensor("ws2b", [1408, 1024], BF16).ap()
    gco_dram = nc.dram_tensor("gco_dram", [16, 64], F32).ap()
    yb_all = nc.dram_tensor("yb_all", [E * CAP, D], BF16).ap()

    taps = {}
    if tap == "x1":
        taps["t_x1"] = nc.dram_tensor("t_x1", [N, D], F32, kind="ExternalOutput").ap()
    if tap == "route":
        taps["t_rwk"] = nc.dram_tensor("t_rwk", [128, 128], F32, kind="ExternalOutput").ap()
        taps["t_raddr"] = nc.dram_tensor("t_raddr", [128, 128], U32, kind="ExternalOutput").ap()
        taps["t_ridxf"] = nc.dram_tensor("t_ridxf", [128, 128], F32, kind="ExternalOutput").ap()
        taps["t_slot"] = nc.dram_tensor("t_slot", [E * CAP, 1], F32, kind="ExternalOutput").ap()
    if tap == "expert":
        taps["t_yb"] = nc.dram_tensor("t_yb", [EPC * CAP, D], BF16, kind="ExternalOutput").ap()
        taps["t_sidx"] = nc.dram_tensor("t_sidx", [128, 32], F32, kind="ExternalOutput").ap()

    c = Ctx()

    class _StopBuild(Exception):
        pass

    def _maybe_stop(name):
        if stop_after == name:
            raise _StopBuild()

    AR_COLS = SBUF_BYTES // 4
    arena_g = nc.alloc_sbuf_tensor("arena", [128, AR_COLS], F32, align_bytes=64)
    A_ = arena_g.ap()
    cursor = [0]

    def carve(cols, dt=F32, at=None, part=128):
        elem = 4 if dt in (F32, U32) else 2
        if at is None:
            off = cursor[0]
            cursor[0] = off + ((cols * elem + 63) // 64) * 64
        else:
            off = at
        assert off % 4 == 0 and off + cols * elem <= SBUF_BYTES, (off, cols, elem)
        f32cols = (cols * elem + 3) // 4
        apv = A_[0:part, off // 4: off // 4 + f32cols]
        if dt == BF16:
            apv = apv.bitcast(BF16)
        elif dt == U32:
            apv = apv.bitcast(U32)
        return apv

    wk = carve(4096)
    stat = carve(64)
    amat = carve(NT * E)
    rwk = carve(NT * 8)
    ridxf = carve(NT * 8)
    rtop = carve(16)
    scnt = carve(E)
    sgco = carve(E)
    gcof = carve(NT * E)
    gateb_b = carve(E)
    boutc = carve(8)
    iota64 = carve(E)
    gbase = carve(E)
    tokid = carve(NT)
    onesr = carve(128)
    onesc = carve(16)
    trix = carve(128)
    identf = carve(128)
    identb = carve(128, BF16)
    permf = carve(128)
    gb1 = carve(E)
    sel = carve(16)
    slotrow = carve(32, U32)
    gw_b = carve(8 * E, BF16)
    ridx = carve(16, U32)
    sidx = carve(32, U32)
    sidxf = carve(32)
    raddr = carve(NT * 8, U32)
    raddr2 = carve(NT * 8, U32)
    slotsf = carve(128)
    zrow = carve(256)
    epsc = carve(16)
    ones2 = carve(64)
    DYN = ((cursor[0] + 63) // 64) * 64
    assert DYN <= 44000, DYN

    xnT = carve(8 * N, BF16, at=DYN + 0)
    wq_db = carve(2 * 1024, BF16, at=DYN + 32768)
    qT = carve(8 * N, BF16, at=DYN + 36864)
    kT = carve(8 * N, BF16, at=DYN + 69632)
    vA = carve(NT * H * 65, BF16, at=DYN + 102400)
    oT = qT
    cosf = carve(N, F32, at=DYN + 102400)
    sins = carve(N, F32, at=DYN + 110592)
    rsc = carve(3 * 1024, F32, at=DYN + 118784)
    gamma_b = carve(D, F32, at=DYN + 36864)
    beta_b = carve(D, F32, at=DYN + 40960)
    df1 = carve(D, F32, at=DYN + 69632)
    wkb = carve(4096, BF16, at=DYN + 135680)
    pcastb = carve(2 * 1408, BF16, at=DYN + 170752)
    shsilu = carve(1024, F32, at=DYN + 144128)
    ws2_b = carve(11 * D, BF16, at=DYN + 148224)
    assert DYN + 176384 <= SBUF_BYTES, (DYN, DYN + 176384)
    wo_b = carve(8 * D, BF16, at=DYN + 0)
    x1T = carve(8 * N, BF16, at=DYN + 0)
    yacc = carve(NT * D, F32, at=DYN + 69632)
    gsh = carve(11 * 1024, BF16, at=DYN + 32768)
    wshdb = carve(2 * 2048, BF16, at=DYN + 55296)
    ew2_db = carve(2 * 6 * D, BF16, at=DYN + 0)
    ew13_db = carve(2 * 2048, BF16, at=DYN + 24576)
    xbT = carve(8 * CAP, BF16, at=DYN + 32768)
    xbT1 = carve(8 * CAP, BF16, at=DYN + 59392)
    xbT2 = carve(8 * CAP, BF16, at=DYN + 135680)
    xbT3 = carve(8 * CAP, BF16, at=DYN + 148224)
    xbT4 = carve(8 * CAP, BF16, at=DYN + 156416)
    hT = carve(6 * CAP, BF16, at=DYN + 40960)
    xbt = carve(2 * 1024, BF16, at=DYN + 47104)
    xbtf = carve(1024, F32, at=DYN + 51200)
    ybs = carve(2 * 1024, BF16, at=DYN + 55296)
    gbuf = carve(4 * 1024, BF16, at=DYN + 0)

    pm1 = nc.alloc_psum_tensor("pm1", [128, 1024], F32).ap()
    pm2 = nc.alloc_psum_tensor("pm2", [128, 1024], F32).ap()
    ppv = nc.alloc_psum_tensor("ppv", [128, 2048], F32).ap()
    def pslot(s):
        return ppv[:, s * 512: s * 512 + 128]

    bufs = {}

    def V(name):
        if name not in bufs:
            bufs[name] = Buf(name)
        return bufs[name]

    pmA = [pm1, pm2]
    pmB = [V("pm1"), V("pm2")]
    ppvB = [V("ppvb0"), V("ppvb1"), V("ppvb2"), V("ppvb3")]

    _regcache = {}

    def bcr(e, v):
        key = v
        if key not in _regcache:
            r = e.alloc_register()
            e.reg_mov(r, v)
            _regcache[key] = r
        return _regcache[key]

    def dma(eng, dst, src, ins=(), outs=(), extra=(), tag="dma", sem=None):
        return c.op(eng, lambda e, d=dst, s=src: e.dma_start(out=d, in_=s),
                    ins=ins, outs=outs, extra=extra, inc=16, tag=tag,
                    sem=(sem if sem is not None else "q_" + tag))

    wkB = [(wk[:, i * 1024:(i + 1) * 1024], V(f"wkR{i}")) for i in range(4)]
    wkbB = [(wkb[:, i * 1024:(i + 1) * 1024], V(f"wkbR{i}")) for i in range(4)]
    stS = [(stat[:, i:i + 1], V(f"st{i}")) for i in range(8)]

    def sinv_ap(t):
        return stat[:, 16 + t:17 + t]

    for dst, src, nm in ((identb, identb_i, "identb"), (identf, identf_i, "identf"),
                         (trix, trix_i, "trix"), (onesr[0:1, 0:128], onesr_i, "onesr"),
                         (permf, permf_i, "permf"),
                         (onesc[:, 0:1], onesc_i, "onesc"), (iota64, iota64_i, "iota64"),
                         (gbase, gbase_i, "gbase"),
                         (tokid, tokid_i, "tokid"), (gb1[0:1, :], gateb_i, "gb1"),
                         (boutc, boutc_i, "boutc"), (sel, sel_i, "sel"),
                         (slotrow[0:32, 0:1], slotrow_i, "slotrow")):
        dma("sync", dst, src, outs=[V(nm)], tag="ldc")
    dma("sync", cosf, cosf_i, outs=[V("cosf")], tag="ldcg", sem="q_ldcg")
    dma("sync", sins, sins_i, outs=[V("sins")], tag="ldcg", sem="q_ldcg")
    dma("sync", wk[0:1, 0:1024], gamma_i, outs=[wkB[0][1]], tag="ldg")
    dma("sync", wk[0:1, 1024:2048], beta_i, outs=[wkB[1][1]], tag="ldb")
    c.finalize_group("q_ldc", *(V(nm) for nm in (
        "identb", "identf", "trix", "onesr", "permf", "onesc", "iota64", "gbase",
        "tokid", "gb1", "boutc", "sel", "slotrow")))
    c.finalize_group("q_ldcg", V("cosf"), V("sins"))

    def bcast_row(dst, dname, src_ap, src_buf, width, psum, pbuf):
        for off in range(0, width, 512):
            w = min(512, width - off)
            c.op("tensor", lambda e, o=off, w_=w, p=psum, s=src_ap: e.matmul(
                p[:, o:o + w_], onesr[0:1, :], s[:, o:o + w_], start=True, stop=True),
                ins=[src_buf, V("onesr")], outs=[pbuf], tag="bc")
        c.op("vector", lambda e, p=psum, d=dst, w_=width: e.tensor_copy(d[:, 0:w_], p[:, 0:w_]),
             ins=[pbuf], outs=[V(dname)])

    bcast_row(gamma_b, "gamma_b", wk[0:1, 0:1024], wkB[0][1], D, pm1, pmB[0])
    bcast_row(beta_b, "beta_b", wk[0:1, 1024:2048], wkB[1][1], D, pm2, pmB[1])
    bcast_row(gateb_b, "gateb_b", gb1[0:1, :], V("gb1"), E, pm1, pmB[0])

    for ic in range(8):
        dma("gpsimd", gw_b[:, ic * E:(ic + 1) * E], gwT[ic * 128:(ic + 1) * 128, :],
            outs=[V(f"gw_{ic}")], tag="ldgw")
    c.finalize_group("q_ldgw", *(V(f"gw_{ic}") for ic in range(8)))

    c.op("vector", lambda e: e.memset(zrow, 0.0), outs=[V("zrow")])
    c.op("vector", lambda e: e.memset(epsc[:, 0:1], 1e-5), outs=[V("epsc")])
    c.op("vector", lambda e: e.memset(ones2, 1.0), outs=[V("ones2")])
    c.op("vector", lambda e: e.memset(epsc[:, 1:2], -EXP_C), outs=[V("negc")])
    spart_b = V("slot_part")
    dma("gpsimd", slot_part.rearrange("(p f) one -> p (f one)", p=128),
        zrow[:, 0:256], ins=[V("zrow")], outs=[spart_b], tag="zslot")

    try:
        _maybe_stop("consts")
        pp_jobs = []
        for ch8 in range(8):
            for hh in range(2):
                pp_jobs.append(("s13", ch8, hh))
        for ch11 in range(11):
            pp_jobs.append(("s2", ch11, 0))
        for el in range(EPC):
            for ch8 in range(8):
                pp_jobs.append(("w13", el, ch8))
            for ch6 in range(6):
                pp_jobs.append(("w2", el, ch6))
        pcs_b = [V("pcs0"), V("pcs1")]
        pcast_b = V("pcast")
        pcsh_b = V("pcast_sh")

        def emit_prepass():
            i = 0
            for kind, el, chn in pp_jobs:
                sl = i % 2
                i += 1
                ob = pcast_b
                if kind == "w13":
                    wsrc = ew13[el][chn * 128:(chn + 1) * 128, :]
                    wdst = ew13b[el][chn * 128:(chn + 1) * 128, :]
                    wid = 1408
                elif kind == "w2":
                    wsrc = ew2T[el][chn * 128:(chn + 1) * 128, :]
                    wdst = ew2b[el][chn * 128:(chn + 1) * 128, :]
                    wid = 1024
                elif kind == "s13":
                    wsrc = ws13[el * 128:(el + 1) * 128, chn * 1408:(chn + 1) * 1408]
                    wdst = ws13b[el * 128:(el + 1) * 128, chn * 1408:(chn + 1) * 1408]
                    wid = 1408
                    ob = pcsh_b
                else:
                    wsrc = ws2T_i[el * 128:(el + 1) * 128, :]
                    wdst = ws2b[el * 128:(el + 1) * 128, :]
                    wid = 1024
                    ob = pcsh_b
                dma("gpsimd", pcastb[:, sl * 1408:sl * 1408 + wid], wsrc,
                    outs=[pcs_b[sl]], tag="ldpc", sem="q_pc%d" % sl)
                dma("gpsimd", wdst, pcastb[:, sl * 1408:sl * 1408 + wid],
                    ins=[pcs_b[sl]], outs=[ob], tag="stpc",
                    sem="q_pc3" if ob is pcsh_b else "q_pc2")

        xnT_b = [[V(f"xnT_{kc}_{it}") for it in range(NT)] for kc in range(8)]
        for it in range(NT if stop_after not in ("A1", "A2") else 2):
            par = it % 2
            xa, bxa = wkB[par]
            dma("sync" if par == 0 else "scalar", xa, x_in[it * 128:(it + 1) * 128, :],
                outs=[bxa], tag="ldx", sem="q_ldx%d" % par)
            sm, bsm = stat[:, 44 + par:45 + par], V(f"ln0sm{par}")
            c.op("vector", lambda e, a=xa, s=sm: e.reduce_sum(s, a, axis=AX.X),
                 ins=[bxa], outs=[bsm])
            c.op("vector", lambda e, s=sm: e.tensor_scalar_mul(s, s, 1.0 / D),
                 ins=[bsm], outs=[bsm])
            df, bdf = (wkB[2][0], V("ln0df0")) if par == 0 else (df1, V("ln0df1"))
            c.op("vector", lambda e, a=xa, d=df, s=sm: e.tensor_scalar(
                d, a, s, None, op0=ALU.subtract), ins=[bxa, bsm], outs=[bdf])
            sq, bsq = stat[:, 46 + par:47 + par], V(f"ln0sq{par}")
            dump, bdump = wkB[3]
            c.op("scalar", lambda e, d=df, du=dump, s=sq: e.activation(
                du, d, AF.Square, accum_out=s), ins=[bdf], outs=[bdump, bsq])
            sr, bsr = stat[:, 48 + par:49 + par], V(f"ln0sr{par}")
            c.op("scalar", lambda e, s=sq, r=sr: e.activation(r, s, AF.Sqrt, bias=epsc[:, 0:1], scale=1.0 / D),
                 ins=[bsq, V("epsc")], outs=[bsr])
            c.op("vector", lambda e, r=sr: e.reciprocal(r, r), ins=[bsr], outs=[bsr])
            c.op("vector", lambda e, d=df, r=sr: e.tensor_scalar(d, d, r, None, op0=ALU.mult),
                 ins=[bdf, bsr], outs=[bdf])
            c.op("vector", lambda e, d=df: e.tensor_mul(d, d, gamma_b),
                 ins=[bdf, V("gamma_b")], outs=[bdf])
            c.op("vector", lambda e, d=df: e.tensor_add(d, d, beta_b),
                 ins=[V("beta_b")], outs=[bdf])
            if stop_after == "A1":
                continue
            for dcj in range(8):
                sl = dcj % 4
                c.op("tensor", lambda e, d=dcj, s=sl, i_=df: e.transpose(
                    pslot(s), i_[:, d * 128:(d + 1) * 128], identf),
                    ins=[bdf, V("identf")], outs=[ppvB[sl]], tag="tpxn")
                c.op("vector", lambda e, d=dcj, s=sl, i=it: e.tensor_copy(
                    xnT[:, d * N + i * 128: d * N + i * 128 + 128], pslot(s)),
                    ins=[ppvB[sl]], outs=[xnT_b[dcj][it]])
        _maybe_stop("A")
        _maybe_stop("A1")
        _maybe_stop("A2")
        qT_b = [[V(f"qT_{ch}_{tf}") for tf in range(2)] for ch in range(8)]
        kT_b = [[V(f"kT_{ch}_{tf}") for tf in range(2)] for ch in range(8)]
        wqs = [V("wqs0"), V("wqs1")]

        for oc in range(16):
            slot = oc % 2
            src = wqkv[:, oc * 128:(oc + 1) * 128].rearrange("(k p) cc -> p k cc", p=128)
            dma("gpsimd", wq_db[:, slot * 1024:(slot + 1) * 1024], src,
                outs=[wqs[slot]], tag="ldwq", sem="q_wq%d" % slot)
            dst_sb = qT if oc < 8 else kT
            dst_bl = qT_b if oc < 8 else kT_b
            ch = oc % 8
            for tf in range(2):
                pb = pmB[tf]
                pa = pmA[tf]
                for hf in range(2):
                    for kc in range(8):
                        c.op("tensor", lambda e, k=kc, f=tf, s=slot, h2=hf, p=pa: e.matmul(
                            p[:, h2 * 512:h2 * 512 + 512],
                            wq_db[:, s * 1024 + k * 128: s * 1024 + k * 128 + 128],
                            xnT[:, k * N + f * 1024 + h2 * 512: k * N + f * 1024 + h2 * 512 + 512],
                            start=(k == 0), stop=(k == 7)),
                            ins=[wqs[slot]] + xnT_b[kc][8 * tf:8 * tf + 8],
                            outs=[pb], tag="qkv")
                sc_ = rsc[:, 0:1024]
                tmp = rsc[:, 1024:2048]
                u = rsc[:, 2048:3072]
                bsc, btmp, bu = V("rscA"), V("rscB"), V("rscC")
                c.op("scalar", lambda e, p=pa, s=sc_: e.activation(s, p[:, 0:1024], AF.Copy),
                     ins=[pb], outs=[bsc], tag="ropecast")
                for h2 in range(2):
                    c.op("tensor", lambda e, s=sc_, hh2=h2: e.matmul(
                        ppv[:, 1024 + hh2 * 512:1024 + hh2 * 512 + 512], permf,
                        s[:, hh2 * 512:hh2 * 512 + 512],
                        start=True, stop=True),
                        ins=[bsc, V("permf")], outs=[ppvB[2 + h2]], tag="ropeswap")
                c.op("vector", lambda e, s=sc_, t=tmp, f=tf: e.tensor_mul(
                    t, s, cosf[:, f * 1024:(f + 1) * 1024]),
                    ins=[bsc, V("cosf")], outs=[btmp])
                c.op("vector", lambda e, uu=u, f=tf: e.tensor_mul(
                    uu, ppv[:, 1024:2048], sins[:, f * 1024:f * 1024 + 1024]),
                    ins=[ppvB[2], ppvB[3], V("sins")], outs=[bu], tag="ropeu")
                c.op("vector", lambda e, t=tmp, uu=u, d=dst_sb, ch_=ch, f=tf: e.tensor_add(
                    d[:, ch_ * N + f * 1024: ch_ * N + f * 1024 + 1024], t, uu),
                    ins=[btmp, bu], outs=[dst_bl[ch][tf]], tag="rope")

        fence_wk = c.fence(*[b for _, b in wkB])
        vw = wk.bitcast(BF16)
        vw_b = [V(f"vw_{kc}") for kc in range(8)]
        for kc in range(8):
            dma("gpsimd", vw[:, kc * 1024:(kc + 1) * 1024],
                wqkv[kc * 128:(kc + 1) * 128, 2048:3072],
                outs=[vw_b[kc]], extra=fence_wk, tag="ldvw")
        c.finalize_group("q_ldvw", *vw_b)
        vA_b = [V(f"vA_{m}") for m in range(NT)]
        fence_rope = c.fence(*(qT_b[ch][tf] for ch in range(8) for tf in range(2)),
                             *(kT_b[ch][tf] for ch in range(8) for tf in range(2)))
        c.op("vector", lambda e: e.memset(vA, 1.0), outs=[V("vA_ones")], extra=fence_rope, tag="vones")
        for m in range(NT):
            pb = pmB[m % 2]
            pa = pmA[m % 2]
            for h2 in range(2):
                for kc in range(8):
                    c.op("tensor", lambda e, k=kc, mm=m, hh2=h2, p=pa: e.matmul(
                        p[:, hh2 * 512:hh2 * 512 + 512],
                        xnT[:, k * N + mm * 128: k * N + mm * 128 + 128],
                        vw[:, k * 1024 + hh2 * 512: k * 1024 + hh2 * 512 + 512],
                        start=(k == 0), stop=(k == 7)),
                        ins=[vw_b[kc], xnT_b[kc][m]], outs=[pb], tag="vmm")
            for h in range(H):
                c.op("vector", lambda e, hh=h, mm=m, p=pa: e.tensor_copy(
                    vA[:, mm * 1040 + hh * 65: mm * 1040 + hh * 65 + 64],
                    p[:, hh * 64:(hh + 1) * 64]),
                    ins=[pb], outs=([vA_b[m]] if h == H - 1 else []),
                    extra=list(V("vA_ones").w), tag="vevac")

        _maybe_stop("B")
        fence_xn_all = c.fence(*(xnT_b[kc][it2] for kc in range(8) for it2 in range(NT)))
        for ic in range(8):
            dma("gpsimd", wo_b[:, ic * D:(ic + 1) * D], wout[ic * 128:(ic + 1) * 128, :],
                outs=[V(f"wo_{ic}")], extra=fence_xn_all, tag="ldwo")
        c.finalize_group("q_ldwo", *(V(f"wo_{ic}") for ic in range(8)))
        emit_prepass()

        oT_b = [[V(f"oT_{h}_{qf}") for qf in range(2)] for h in range(H)]
        pm1hB = [V("pm1h0"), V("pm1h1")]
        pm2hB = [V("pm2h0"), V("pm2h1")]
        pAslot = [(wkb[:, s * 512:s * 512 + 512], V(f"PA{s}")) for s in range(3)]
        pBslot = [(wkb[:, 1536 + s * 512:1536 + s * 512 + 512], V(f"PB{s}")) for s in range(3)]
        btm = wkb[:, 3072:3584]
        bbtm = V("btmS")

        pairs = [(pm1, pm1hB[0]), (pm2, pm2hB[0])]
        pslotP = [(wkb[:, s * 1024:s * 1024 + 1024], V(f"PP{s}")) for s in range(3)]

        def emit_S(hp, qf4, kc, first):
            inq = [qT_b[hp][qf4 // 2], kT_b[hp][kc // 8]]
            pt, ptb = pairs[kc % 2]
            c.op("tensor", lambda e, k=kc, h_=hp, q4=qf4, d=pt: e.matmul(
                d[:, 0:512], kT[0:64, h_ * N + k * 128: h_ * N + k * 128 + 128],
                qT[0:64, h_ * N + q4 * 512: h_ * N + q4 * 512 + 512],
                start=True, stop=True, tile_position=(0, 0)),
                ins=inq, outs=[ptb], tag="sA",
                extra=(c.fence(pmB[0], pmB[1]) if first else ()))
            c.op("tensor", lambda e, k=kc, h_=hp, q4=qf4, d=pt: e.matmul(
                d[:, 512:1024], kT[64:128, h_ * N + k * 128: h_ * N + k * 128 + 128],
                qT[64:128, h_ * N + q4 * 512: h_ * N + q4 * 512 + 512],
                start=True, stop=True, tile_position=(64, 0)),
                ins=inq, outs=[ptb], tag="sB")

        def emit_exp(kc):
            sl = kc % 3
            pt, ptb = pairs[kc % 2]
            c.op("scalar", lambda e, d=pslotP[sl][0], s=pt: e.activation(
                d, s[:, 0:1024], AF.Exp, scale=0.125, bias=epsc[:, 1:2]),
                ins=[ptb, V("negc")], outs=[pslotP[sl][1]], tag="expAB")

        def emit_pv(hp, k1, cp):
            sl = k1 % 3
            base = cp * 1024
            c.op("tensor", lambda e, k=k1, h_=hp, d=pslotP[sl][0], b=base: e.matmul(
                ppv[0:65, b:b + 512],
                vA[:, k * 1040 + (2 * h_) * 65: k * 1040 + (2 * h_) * 65 + 65],
                d[:, 0:512], start=(k == 0), stop=(k == NT - 1)),
                ins=[pslotP[sl][1], vA_b[k1]], outs=[ppvB[2 * cp]], tag="pvA")
            c.op("tensor", lambda e, k=k1, h_=hp, d=pslotP[sl][0], b=base: e.matmul(
                ppv[0:65, b + 512:b + 1024],
                vA[:, k * 1040 + (2 * h_ + 1) * 65: k * 1040 + (2 * h_ + 1) * 65 + 65],
                d[:, 512:1024], start=(k == 0), stop=(k == NT - 1)),
                ins=[pslotP[sl][1], vA_b[k1]], outs=[ppvB[2 * cp + 1]], tag="pvB")

        den, bden = wkB[3]
        bc_sb, bbc = wkB[2]
        dden_b = V("dden")

        def emit_ep1(hp, qf4, cp):
            b = cp * 1024
            c.op("vector", lambda e, b_=b: e.reciprocal(
                den[64:65, 0:512], ppv[64:65, b_:b_ + 512]),
                ins=[ppvB[2 * cp]], outs=[bden], tag="recip")
            c.op("vector", lambda e, b_=b: e.reciprocal(
                den[64:65, 512:1024], ppv[64:65, b_ + 512:b_ + 1024]),
                ins=[ppvB[2 * cp + 1]], outs=[bden], tag="recip")
            dma("sync", dden, den[64:65, 0:1024], ins=[bden], outs=[dden_b],
                tag="stden", sem="q_den")

        def emit_ep2(hp, qf4, cp):
            b = cp * 1024
            exo = c.fence(qT_b[hp][qf4 // 2])
            dma("sync", bc_sb[0:64, 0:512], dden[0:1, 0:512].broadcast_to((64, 512)),
                ins=[dden_b], outs=[bbc], tag="ldbca", sem="q_bca")
            c.op("vector", lambda e, h_=hp, q4=qf4, b_=b: e.tensor_mul(
                oT[0:64, h_ * N + q4 * 512:h_ * N + q4 * 512 + 512],
                ppv[0:64, b_:b_ + 512], bc_sb[0:64, 0:512]),
                ins=[ppvB[2 * cp], bbc], outs=[oT_b[2 * hp][qf4 // 2]], extra=exo, tag="onormA")
            dma("sync", bc_sb[0:64, 512:1024], dden[0:1, 512:1024].broadcast_to((64, 512)),
                ins=[dden_b], outs=[bbc], tag="ldbcb", sem="q_bcb")

        def emit_ep3(hp, qf4, cp):
            b = cp * 1024
            c.op("vector", lambda e, b_=b: e.tensor_mul(
                btm[0:64, 0:512], ppv[0:64, b_ + 512:b_ + 1024], bc_sb[0:64, 512:1024]),
                ins=[ppvB[2 * cp + 1], bbc], outs=[bbtm], tag="onormBm")
            dma("sync", oT[64:128, hp * N + qf4 * 512:hp * N + qf4 * 512 + 512],
                btm[0:64, 0:512], ins=[bbtm], outs=[oT_b[2 * hp + 1][qf4 // 2]],
                extra=c.fence(qT_b[hp][qf4 // 2]), tag="shB", sem="q_shb")

        chunks = [(hp, qf4) for hp in range(8) for qf4 in range(4)]
        prev = None
        for ci, (hp, qf4) in enumerate(chunks):
            cp = ci % 2
            for kc in range(NT):
                emit_S(hp, qf4, kc, first=(ci == 0 and kc == 0))
                if prev is not None:
                    if kc == 1:
                        emit_ep1(*prev)
                    elif kc == 2:
                        emit_ep2(*prev)
                    elif kc == 3:
                        emit_ep3(*prev)
                if kc >= 3:
                    emit_pv(hp, kc - 3, cp)
                emit_exp(kc)
            for k1 in (NT - 3, NT - 2, NT - 1):
                emit_pv(hp, k1, cp)
            prev = (hp, qf4, cp)
        emit_ep1(*prev)
        emit_ep2(*prev)
        emit_ep3(*prev)

        _maybe_stop("S2")
        wsh_b = [V("wsh0"), V("wsh1")]
        fence_attnspan = c.fence(*(oT_b[h][qf] for h in range(H) for qf in range(2)))
        fence_wo = c.fence(*(V(f"wo_{ic}") for ic in range(8)))
        for ik in range(11):
            dma("sync", ws2_b[:, ik * D:(ik + 1) * D], ws2b[ik * 128:(ik + 1) * 128, :],
                outs=[V(f"ws2_{ik}")],
                extra=list(fence_attnspan) + (list(c.fence(pcsh_b)) if ik == 0 else []), tag="ldws2")
        c.finalize_group("q_ldws2", *(V(f"ws2_{ik}") for ik in range(11)))
        ytile_b = [V(f"yt_{it}") for it in range(NT)]
        for dc in range(8):
            for tf in range(2):
                pb = pmB[(dc * 2 + tf) % 2]
                pa = pmA[(dc * 2 + tf) % 2]
                for h2 in range(2):
                    for ic in range(8):
                        c.op("tensor", lambda e, i=ic, d=dc, f=tf, hh2=h2, p=pa: e.matmul(
                            p[:, hh2 * 512:hh2 * 512 + 512],
                            wo_b[:, i * D + d * 128: i * D + d * 128 + 128],
                            oT[:, i * N + f * 1024 + hh2 * 512: i * N + f * 1024 + hh2 * 512 + 512],
                            start=(i == 0), stop=(i == 7)),
                            ins=[V(f"wo_{ic}"), oT_b[2 * ic][tf], oT_b[2 * ic + 1][tf]],
                            outs=[pb], tag="proj",
                            extra=(c.fence(pm1hB[0], pm1hB[1], pm2hB[0], pm2hB[1])
                                   if dc == 0 and tf == 0 and h2 == 0 and ic == 0 else ()))
                pj, bpj = wkB[tf % 2]
                c.op("scalar", lambda e, p=pa, d=pj, dcc=dc: e.activation(
                    d, p[:, 0:1024], AF.Identity, bias=boutc[:, dcc:dcc + 1]),
                    ins=[pb, V("boutc")], outs=[bpj], tag="projevac")
                for tt in range(8):
                    it = tf * 8 + tt
                    sl = tt % 2
                    c.op("tensor", lambda e, s=sl, d=pj, t=tt: e.transpose(
                        pslot(s), d[:, t * 128:(t + 1) * 128], identf),
                        ins=[bpj, V("identf")], outs=[ppvB[sl]], tag="tppj")
                    c.op("vector", lambda e, s=sl, i=it, dcc=dc: e.tensor_copy(
                        yacc[:, i * D + dcc * 128: i * D + dcc * 128 + 128], pslot(s)),
                        ins=[ppvB[sl]], outs=([ytile_b[it]] if dc == 7 else []),
                        extra=(c.fence(ytile_b[it]) if dc == 7 else ()), tag="pjst")

        fence_oTread = c.fence(*(oT_b[h][qf] for h in range(H) for qf in range(2)))
        for sj0 in range(2):
            srcap0 = ws13b[:, sj0 * 256:(sj0 + 1) * 256].rearrange("(k p) cc -> p k cc", p=128)
            dma("sync", wshdb[:, sj0 * 2048:(sj0 + 1) * 2048], srcap0,
                outs=[wsh_b[sj0]], extra=fence_oTread, tag="ldwsh", sem="q_wsh%d" % sj0)

        amat_b = [V(f"amat_{it}") for it in range(NT)]
        ridxf_b = [V(f"ridxf_{it}") for it in range(NT)]
        rwk_b = [V(f"rwk_{it}") for it in range(NT)]
        sinv_b = [V(f"sinv_{it}") for it in range(NT)]
        gcof_b = V("gcof")

        wk1 = wk[:, 1024:2048]

        def emit_r1_tile(it):
            par = it % 2
            rb = ppvB[2 + par]
            pv64 = ppv[:, 1024 + par * 512:1024 + par * 512 + 64]
            lg, blg = wk1[:, par * 64:par * 64 + 64], V(f"r1lg{par}")
            bi, bbi = wk1[:, 128 + par * 64:128 + par * 64 + 64], V(f"r1bi{par}")
            oh, boh = wk1[:, 256 + par * 64:256 + par * 64 + 64], V(f"r1oh{par}")
            wtmp, bwt = wk1[:, 384 + par * 64:384 + par * 64 + 64], V(f"r1wt{par}")
            nm_, bnm = stat[:, 38 + par:39 + par], V(f"r1nm{par}")
            sm, bsm = stat[:, 40 + par:41 + par], V(f"r1sm{par}")
            sv, bsv = stat[:, 42 + par:43 + par], V(f"r1sv{par}")
            rtp, brtp = rtop[:, par * 8:par * 8 + 8], V(f"rtop{par}")
            rix, brix = ridx[:, par * 8:par * 8 + 8], V(f"ridx{par}")
            for ic in range(8):
                c.op("tensor", lambda e, i=ic, t=it, p=pv64: e.matmul(
                    p, x1T[:, i * N + t * 128: i * N + t * 128 + 128],
                    gw_b[:, i * E:(i + 1) * E], start=(i == 0), stop=(i == 7)),
                    ins=[V(f"gw_{ic}"), x1T_b[ic][it]], outs=[rb], tag="gate")
            c.op("vector", lambda e, p=pv64, s=nm_: e.reduce_max(s, p, axis=AX.X, negate=True),
                 ins=[rb], outs=[bnm], tag="gmax")
            c.op("scalar", lambda e, p=pv64, s=nm_, a=sm, d=lg: e.activation(
                d, p, AF.Exp, bias=s, accum_out=a),
                ins=[rb, bnm], outs=[blg, bsm],
                extra=(c.fence(wkB[1][1]) if it == 0 else ()), tag="gexp")
            c.op("vector", lambda e, a=sm, s=sv: e.reciprocal(s, a), ins=[bsm], outs=[bsv])
            c.op("vector", lambda e, s=sv, t=it: e.tensor_copy(sinv_ap(t), s),
                 ins=[bsv], outs=[sinv_b[it]])
            c.op("vector", lambda e, s=sv, d=bi, l_=lg: e.scalar_tensor_tensor(
                d, l_, s, gateb_b,
                op0=ALU.mult, op1=ALU.add),
                ins=[blg, bsv, V("gateb_b")], outs=[bbi],
                extra=(c.fence(wkB[1][1]) if it == 0 else ()), tag="bias")
            c.op("vector", lambda e, d=bi, r=rtp: e.max(out=r, in_=d),
                 ins=[bbi], outs=[brtp], tag="top8")
            c.op("vector", lambda e, d=bi, r=rtp, x=rix: e.max_index(out=x, in_max=r, in_values=d),
                 ins=[brtp, bbi], outs=[brix], tag="idx8")
            c.op("vector", lambda e, t=it, x=rix: e.tensor_copy(ridxf[:, t * 8:t * 8 + 8], x),
                 ins=[brix], outs=[ridxf_b[it]], tag="idxf")
            for k in range(TOPK):
                c.op("vector", lambda e, t=it, kk=k, d=oh: e.tensor_scalar(
                    d, iota64, ridxf[:, t * 8 + kk:t * 8 + kk + 1], None, op0=ALU.is_equal),
                    ins=[ridxf_b[it], V("iota64")], outs=[boh], tag="oneh")
                if k == 0:
                    c.op("vector", lambda e, t=it, d=oh: e.tensor_copy(
                        amat[:, t * E:(t + 1) * E], d),
                        ins=[boh], outs=[amat_b[it]])
                else:
                    c.op("vector", lambda e, t=it, d=oh: e.tensor_add(
                        amat[:, t * E:(t + 1) * E], amat[:, t * E:(t + 1) * E], d),
                        ins=[boh], outs=[amat_b[it]])
                c.op("vector", lambda e, d=oh, w=wtmp, l_=lg: e.tensor_mul(
                    w, d, l_),
                    ins=[boh, blg], outs=[bwt])
                c.op("vector", lambda e, t=it, kk=k, w=wtmp: e.reduce_sum(
                    rwk[:, t * 8 + kk:t * 8 + kk + 1], w, axis=AX.X),
                    ins=[bwt], outs=[rwk_b[it]], tag="wk")
            c.op("tensor", lambda e, t=it, p=pv64: e.matmul(
                p[0:1, 0:64], onesc[:, 0:1], amat[:, t * E:(t + 1) * E], start=True, stop=True),
                ins=[amat_b[it], V("onesc")], outs=[rb], tag="tot")
            c.op("vector", lambda e, t=it, p=pv64: e.tensor_copy(gcof[0:1, t * E:(t + 1) * E], p[0:1, :]),
                 ins=[rb], outs=[gcof_b], tag="totcp")

        x1T_b = [[V(f"x1T_{kc}_{it}") for it in range(NT)] for kc in range(8)]
        x1m_b = V("x1_mine")

        dump = wk[:, 3 * 1024:4 * 1024]
        bdump = V("lndump")

        def plain_ln(ya, yab, tag, par=0, first=False):
            sm, bsm = stat[:, 32 + par:33 + par], V(f"lnsm{par}")
            c.op("vector", lambda e, y=ya, s=sm: e.reduce_sum(s, y, axis=AX.X),
                 ins=[yab], outs=[bsm])
            c.op("vector", lambda e, s=sm: e.tensor_scalar_mul(s, s, 1.0 / D),
                 ins=[bsm], outs=[bsm])
            c.op("vector", lambda e, y=ya, s=sm: e.tensor_scalar(y, y, s, None, op0=ALU.subtract),
                 ins=[bsm], outs=[yab])
            sq, bsq = stat[:, 34 + par:35 + par], V(f"lnsq{par}")
            c.op("scalar", lambda e, y=ya, du=dump, s=sq: e.activation(
                du, y, AF.Square, accum_out=s), ins=[yab], outs=[bdump, bsq],
                extra=(c.fence(wkB[3][1]) if first else ()))
            sr, bsr = stat[:, 36 + par:37 + par], V(f"lnsr{par}")
            c.op("scalar", lambda e, s=sq, r=sr: e.activation(r, s, AF.Sqrt, bias=epsc[:, 0:1], scale=1.0 / D),
                 ins=[bsq, V("epsc")], outs=[bsr])
            c.op("vector", lambda e, r=sr: e.reciprocal(r, r), ins=[bsr], outs=[bsr])
            c.op("vector", lambda e, y=ya, r=sr: e.tensor_scalar(y, y, r, None, op0=ALU.mult),
                 ins=[bsr], outs=[yab], tag=tag)

        for it in range(NT):
            par = it % 2
            xa = wk[:, 2 * 1024:3 * 1024] if par == 0 else wk[:, 0:1024]
            bxa = V(f"xar{par}")
            dma("scalar", xa, x_in[it * 128:(it + 1) * 128, :], outs=[bxa],
                extra=(c.fence(wkB[2 if par == 0 else 0][1]) if it < 2 else ()), tag="ldx2")
            ya = yacc[:, it * D:(it + 1) * D]
            yab = ytile_b[it]
            c.op("vector", lambda e, a=xa, y=ya: e.tensor_add(y, y, a),
                 ins=[bxa], outs=[yab], tag="resid")
            plain_ln(ya, yab, "ln1", par=par, first=(it == 0))
            for dcj in range(8):
                sl = dcj % 2
                c.op("tensor", lambda e, d=dcj, s=sl, y=ya: e.transpose(
                    pslot(s), y[:, d * 128:(d + 1) * 128], identf),
                    ins=[yab, V("identf")], outs=[ppvB[sl]], tag="tpx1")
                c.op("vector", lambda e, d=dcj, s=sl, i=it: e.tensor_copy(
                    x1T[:, d * N + i * 128: d * N + i * 128 + 128], pslot(s)),
                    ins=[ppvB[sl]], outs=[x1T_b[dcj][it]])
            dma("gpsimd", x1_mine[it * 128:(it + 1) * 128, :], ya,
                ins=[yab], outs=[x1m_b], tag="x1st")
            if tap == "x1":
                dma("scalar", taps["t_x1"][it * 128:(it + 1) * 128, :], ya,
                    ins=[yab], outs=[V("tapx1")], tag="tapx1")

        x1a_b = V("x1_all")

        cntall_b = V("cnt_all")

        def emit_collectives():
            dma("gpsimd", cnt_mine.rearrange("t e -> (t e)").rearrange("(one f) -> one f", one=1),
                gcof[0:1, :], ins=[gcof_b], outs=[V("cnt_mine")], tag="cntst")
            c.op("gpsimd", lambda e: e.collective_compute(
                "AllGather", ALU.bypass, replica_groups=[list(range(NC_))],
                ins=[cnt_mine], outs=[cnt_all]),
                ins=[V("cnt_mine")], outs=[cntall_b], inc=1, tag="agcnt")
            c.op("gpsimd", lambda e: e.collective_compute(
                "AllGather", ALU.bypass, replica_groups=[list(range(NC_))],
                ins=[x1_mine], outs=[x1_all]),
                ins=[x1m_b], outs=[x1a_b], inc=1, tag="agx1")

        def emit_r2():
            dma("sync", scnt, cnt_all, ins=[cntall_b], outs=[V("scnt")], tag="ldcnt")
            c.op("tensor", lambda e: e.matmul(ppv[:, 1024:1088], trix, scnt, start=True, stop=True),
                 ins=[V("scnt"), V("trix")], outs=[ppvB[2]], tag="gcum")
            c.op("vector", lambda e: e.tensor_copy(sgco, ppv[:, 1024:1088]), ins=[ppvB[2]], outs=[V("sgco")])
            c.op("tensor", lambda e: e.matmul(ppv[0:16, 1536:1600], sel, sgco, start=True, stop=True),
                 ins=[V("sgco"), V("sel")], outs=[ppvB[3]], tag="selrows")
            myco, bmyco = wkB[3]
            c.op("vector", lambda e: e.tensor_copy(myco[0:16, 0:64], ppv[0:16, 1536:1600]),
                 ins=[ppvB[3]], outs=[bmyco])
            dma("sync", gco_dram, myco[0:16, 0:64], ins=[bmyco], outs=[V("gco_dram")], tag="gcodma1")
            dma("sync", gcof[0:1, :], gco_dram.rearrange("p f -> (p f)").rearrange("(one f) -> one f", one=1),
                ins=[V("gco_dram")], outs=[gcof_b], tag="gcodma2")

        raddr_b = [V(f"raddr_{it}") for it in range(NT)]
        last_scat = [None]

        def emit_r3_tile(it):
            par = it % 2
            rb = ppvB[2 + par]
            pv64 = ppv[:, 1024 + par * 512:1024 + par * 512 + 64]
            c.op("tensor", lambda e, t=it, p=pv64: e.matmul(
                p, trix, amat[:, t * E:(t + 1) * E], start=True, stop=False),
                ins=[amat_b[it], V("trix")], outs=[rb], tag="cpos1")
            c.op("tensor", lambda e, t=it, p=pv64: e.matmul(
                p, onesr[0:1, :], gcof[0:1, t * E:(t + 1) * E], start=False, stop=True),
                ins=[gcof_b, V("onesr")], outs=[rb], tag="cpos2")
            oh, boh = wk1[:, 512 + par * 64:512 + par * 64 + 64], V(f"r3oh{par}")
            ps_, bps = wk1[:, 640 + par * 128:640 + par * 128 + 128], V(f"r3ps{par}")
            cpv, bcpv = wk[:, par * 64:par * 64 + 64], V(f"r3cp{par}")
            c.op("vector", lambda e, p=pv64, d=cpv: e.tensor_copy(d, p),
                 ins=[rb], outs=[bcpv], tag="cposcp")
            for k in range(TOPK):
                c.op("vector", lambda e, t=it, kk=k, d=oh: e.tensor_scalar(
                    d, iota64, ridxf[:, t * 8 + kk:t * 8 + kk + 1], None, op0=ALU.is_equal),
                    ins=[ridxf_b[it], V("iota64")], outs=[boh], tag="oneh2")
                c.op("vector", lambda e, d=oh, p=cpv, w=ps_: e.tensor_mul(
                    w[:, 0:64], d, p),
                    ins=[boh, bcpv], outs=[bps])
                c.op("vector", lambda e, kk=k, w=ps_: e.reduce_sum(
                    w[:, 64 + kk:65 + kk], w[:, 0:64], axis=AX.X),
                    ins=[bps], outs=[bps], tag="posk")
            c.op("vector", lambda e, w=ps_: e.tensor_scalar(
                w[:, 72:78], w[:, 64:70], 512.0, None, op0=ALU.is_lt),
                ins=[bps], outs=[bps], tag="keep")
            c.op("vector", lambda e, t=it, w=ps_: e.tensor_mul(
                rwk[:, t * 8:t * 8 + 6], rwk[:, t * 8:t * 8 + 6], w[:, 72:78]),
                ins=[bps], outs=[rwk_b[it]], tag="wmask")
            c.op("vector", lambda e, t=it: e.tensor_scalar(
                rwk[:, t * 8:t * 8 + 6], rwk[:, t * 8:t * 8 + 6], sinv_ap(t), None, op0=ALU.mult),
                ins=[sinv_b[it]], outs=[rwk_b[it]], tag="wnorm")
            c.op("vector", lambda e, t=it, w=ps_: e.tensor_scalar(
                w[:, 80:86], ridxf[:, t * 8:t * 8 + 6], 512.0, None, op0=ALU.mult),
                ins=[ridxf_b[it]], outs=[bps], tag="addr1")
            c.op("vector", lambda e, w=ps_: e.tensor_add(
                w[:, 80:86], w[:, 80:86], w[:, 64:70]),
                ins=[bps], outs=[bps], tag="addr2")
            c.op("vector", lambda e, w=ps_: e.tensor_scalar(
                w[:, 88:94], w[:, 72:78], -1e6, 1e6, op0=ALU.mult, op1=ALU.add),
                ins=[bps], outs=[bps], tag="addr3")
            c.op("vector", lambda e, w=ps_: e.tensor_add(
                w[:, 80:86], w[:, 80:86], w[:, 88:94]),
                ins=[bps], outs=[bps], tag="addr4")
            c.op("vector", lambda e, t=it, w=ps_: e.tensor_copy(
                raddr[:, t * 8:t * 8 + 6], w[:, 80:86]),
                ins=[bps], outs=[raddr_b[it]], tag="addru")
            oh2, boh2 = oh, boh
            ge, bge = wk1[:, 896 + par * 64:896 + par * 64 + 64], V(f"r3ge{par}")
            for k in range(TOPK):
                c.op("vector", lambda e, t=it, kk=k, d=oh2: e.tensor_scalar(
                    d, iota64, ridxf[:, t * 8 + kk:t * 8 + kk + 1], None, op0=ALU.is_equal),
                    ins=[ridxf_b[it], V("iota64")], outs=[boh2], tag="ga_oneh")
                c.op("vector", lambda e, d=oh2, g=ge: e.tensor_mul(
                    g, d, gbase),
                    ins=[boh2, V("gbase")], outs=[bge], tag="ga_mul")
                c.op("vector", lambda e, kk=k, g=ge, w=ps_: e.reduce_sum(
                    w[:, 96 + kk:97 + kk], g, axis=AX.X),
                    ins=[bge], outs=[bps], tag="ga_red")
            c.op("vector", lambda e, w=ps_: e.tensor_add(
                w[:, 96:102], w[:, 96:102], w[:, 64:70]),
                ins=[bps], outs=[bps], tag="ga_s3")
            c.op("vector", lambda e, w=ps_: e.tensor_add(
                w[:, 96:102], w[:, 96:102], w[:, 88:94]),
                ins=[bps], outs=[bps], tag="ga_oob")
            c.op("vector", lambda e, t=it, w=ps_: e.tensor_copy(
                raddr2[:, t * 8:t * 8 + 6], w[:, 96:102]),
                ins=[bps], outs=[V(f"raddr2_{it}")], tag="ga_u")
            for k in range(TOPK):
                last_scat[0] = c.op("gpsimd", lambda e, t=it, kk=k: e.indirect_dma_start(
                    out=slot_part,
                    out_offset=IndirectOffsetOnAxis(raddr[:, t * 8 + kk:t * 8 + kk + 1], 0),
                    in_=tokid[:, t:t + 1], in_offset=None,
                    bounds_check=bcr(e, E * CAP - 1), oob_is_err=False),
                    ins=[raddr_b[it], V("tokid")], outs=[],
                    extra=list(spart_b.w), inc=16, tag="scat", sem="q_scat")

        gsh_b = [V(f"gsh_{j}") for j in range(11)]
        r1_done = [0]
        for tq in range(2):
            for sj in range(11):
                slot = sj % 2
                if not (tq == 0 and sj < 2):
                    srcap = ws13b[:, sj * 256:(sj + 1) * 256].rearrange("(k p) cc -> p k cc", p=128)
                    dma("sync", wshdb[:, slot * 2048:(slot + 1) * 2048], srcap,
                        outs=[wsh_b[slot]], extra=fence_attnspan, tag="ldwsh", sem="q_wsh%d" % slot)
                for half in range(2):
                    pa = pmA[half]
                    pb = pmB[half]
                    for h2 in range(2):
                        for kc in range(8):
                            c.op("tensor", lambda e, k=kc, s=slot, q=tq, hf=half, hh2=h2, p=pa: e.matmul(
                                p[:, hh2 * 512:hh2 * 512 + 512],
                                wshdb[:, s * 2048 + k * 256 + hf * 128: s * 2048 + k * 256 + hf * 128 + 128],
                                x1T[:, k * N + q * 1024 + hh2 * 512: k * N + q * 1024 + hh2 * 512 + 512],
                                start=(k == 0), stop=(k == 7)),
                                ins=[wsh_b[slot]] + [x1T_b[kc][tq * 8 + itt] for itt in range(8)],
                                outs=[pb], tag="sh13")
                c.op("scalar", lambda e: e.activation(shsilu[:, 0:1024], pm1[:, 0:1024], AF.Silu),
                     ins=[pmB[0]], outs=[V("shsilu")], tag="shsilu")
                c.op("vector", lambda e, j=sj, q=tq: e.tensor_mul(
                    gsh[:, j * 1024:(j + 1) * 1024], shsilu[:, 0:1024], pm2[:, 0:1024]),
                    ins=[V("shsilu"), pmB[1]], outs=[gsh_b[sj]], tag="shmul")
                if tq == 0:
                    while r1_done[0] < (sj + 1) * NT // 11:
                        emit_r1_tile(r1_done[0])
                        r1_done[0] += 1
            if tq == 0:
                emit_collectives()
                emit_r2()
            for mt in range(8):
                m = tq * 8 + mt
                pb = pmB[mt % 2]
                pa = pmA[mt % 2]
                for h2 in range(2):
                    for ik in range(11):
                        c.op("tensor", lambda e, i=ik, mm=mt, q=tq, hh2=h2, p=pa: e.matmul(
                            p[:, hh2 * 512:hh2 * 512 + 512],
                            gsh[:, i * 1024 + mm * 128: i * 1024 + mm * 128 + 128],
                            ws2_b[:, i * D + hh2 * 512: i * D + hh2 * 512 + 512],
                            start=(i == 0), stop=(i == 10)),
                            ins=[V(f"ws2_{ik}"), gsh_b[ik]], outs=[pb], tag="zmm")
                ya = yacc[:, m * D:(m + 1) * D]
                c.op("vector", lambda e, p=pa, y=ya: e.tensor_add(y, y, p[:, 0:1024]),
                     ins=[pb], outs=[ytile_b[m]], tag="zadd")
                if tq == 1:
                    emit_r3_tile(2 * mt)
                    emit_r3_tile(2 * mt + 1)

        _maybe_stop("SH")
        if tap == "route":
            dma("scalar", taps["t_rwk"], rwk, ins=list(rwk_b), outs=[V("taprwk")], tag="tapr")
            dma("scalar", taps["t_raddr"], raddr, ins=list(raddr_b), outs=[V("tapraddr")], tag="tapr")
            dma("scalar", taps["t_ridxf"], ridxf, ins=list(ridxf_b), outs=[V("tapridxf")], tag="tapr")

        slota_b = V("slot_all")
        c.op("gpsimd", lambda e: e.collective_compute(
            "AllReduce", ALU.add, replica_groups=[list(range(NC_))],
            ins=[slot_part], outs=[slot_all]),
            ins=[spart_b], outs=[slota_b], extra=[last_scat[0]], inc=1, tag="arslot")
        if tap == "route":
            dma("scalar", taps["t_slot"], slot_all, ins=[slota_b], outs=[V("tapslot")], tag="tapr")

        _maybe_stop("R")
        fence_x1T = c.fence(*(x1T_b[kc][it] for kc in range(8) for it in range(NT)))
        fence_gshspan = c.fence(*gsh_b, wsh_b[0], wsh_b[1])
        c.op("gpsimd", lambda e: e.indirect_dma_start(
            out=slotsf[0:32, 0:128], out_offset=None,
            in_=slot_all.rearrange("(r f) one -> r (f one)", f=128),
            in_offset=IndirectOffsetOnAxis(slotrow[0:32, 0:1], 0),
            bounds_check=bcr(e, E * CAP // 128 - 1), oob_is_err=False),
            ins=[slota_b, V("slotrow")], outs=[V("slotsf")], inc=16, tag="ldslot", sem="q_ldslot")
        c.op("tensor", lambda e: e.transpose(ppv[:, 0:32], slotsf[0:32, 0:128], identf[0:32, 0:32]),
             ins=[V("slotsf"), V("identf")], outs=[ppvB[0]], tag="tpslot")
        c.op("vector", lambda e: e.tensor_copy(sidxf, ppv[:, 0:32]),
             ins=[ppvB[0]], outs=[V("sidxf")])
        msk, bmsk = wkB[0]
        c.op("vector", lambda e, m=msk: e.tensor_scalar(m[:, 0:32], sidxf, 0.0, 1e6,
                                                        op0=ALU.is_equal, op1=ALU.mult),
             ins=[V("sidxf")], outs=[bmsk],
             extra=c.fence(V("r3cp0"), V("r3cp1")))
        c.op("vector", lambda e: e.tensor_scalar_add(sidxf, sidxf, -2.0),
             ins=[], outs=[V("sidxf")], extra=c.fence(V("sidxf"), bmsk), tag="sidxm2")
        c.op("vector", lambda e, m=msk: e.tensor_add(sidxf, sidxf, m[:, 0:32]),
             ins=[bmsk], outs=[V("sidxf")])
        c.op("vector", lambda e: e.tensor_copy(sidx, sidxf),
             ins=[], outs=[V("sidx")], extra=c.fence(V("sidxf")), tag="sidxu")
        if tap == "expert":
            dma("scalar", taps["t_sidx"], sidxf, ins=[V("sidxf")], outs=[V("tapsidx")], tag="tape")

        ew13_s = [V("ew13s0"), V("ew13s1")]
        ew2_s = [V("ew2s0"), V("ew2s1")]
        hT_b = V("hT")
        xbTa = [xbT, xbT1, xbT2, xbT3, xbT4]
        xbT_b = [V(f"xbTs{sl}") for sl in range(5)]
        c.op("vector", lambda e: e.memset(hT[64:128, 5 * 512:6 * 512], 0.0),
             outs=[V("hTz")], extra=fence_gshspan, tag="hTz")
        c.op("vector", lambda e: e.memset(xbt[:, 0:1024], 0.0), outs=[V("xbt0")],
             extra=fence_x1T + fence_gshspan, tag="xbtz")
        c.op("vector", lambda e: e.memset(xbt[:, 1024:2048], 0.0), outs=[V("xbt1")],
             extra=fence_x1T + fence_gshspan, tag="xbtz")
        ybm_b2 = [V("yb_mine0"), V("yb_mine1"), V("yb_mine2")]
        ych_start = {0: 0, 1: 0, 2: 1, 3: 1, 4: 1, 5: 1, 6: 2, 7: 2}
        ybs_b = [V("ybs0"), V("ybs1")]
        xbt_bufs = [V("xbt0"), V("xbt1")]

        def emit_gather(el):
            sl5 = el % 5
            for sc2 in range(4):
                xsl = (el * 4 + sc2) % 2
                c.op("gpsimd", lambda e, l=el, s=sc2, x=xsl: e.indirect_dma_start(
                    out=xbt[:, x * 1024:x * 1024 + 1024], out_offset=None,
                    in_=x1_all, in_offset=IndirectOffsetOnAxis(sidx[:, l * 4 + s:l * 4 + s + 1], 0),
                    bounds_check=bcr(e, B * N - 1), oob_is_err=False),
                    ins=[x1a_b, V("sidx")], outs=[xbt_bufs[xsl]], inc=16, tag="gxb",
                    sem="q_gxb%d" % xsl,
                    extra=(fence_gshspan if el == 0 and sc2 < 2 else ()))
                c.op("scalar", lambda e, x=xsl: e.activation(
                    xbtf[:, 0:1024], xbt[:, x * 1024:x * 1024 + 1024], AF.Copy),
                    ins=[xbt_bufs[xsl]], outs=[V("xbtf")], tag="xbcast")
                for dcj in range(8):
                    sl = dcj % 4
                    c.op("tensor", lambda e, d=dcj, s=sl: e.transpose(
                        pslot(s), xbtf[:, d * 128:(d + 1) * 128], identf),
                        ins=[V("xbtf"), V("identf")], outs=[ppvB[sl]], tag="tpxb")
                    c.op("vector", lambda e, d=dcj, s=sl, ss=sc2, t5=sl5: e.tensor_copy(
                        xbTa[t5][:, d * CAP + ss * 128: d * CAP + ss * 128 + 128], pslot(s)),
                        ins=[ppvB[sl]], outs=[xbT_b[sl5]],
                        extra=(fence_gshspan if el == 0 and sc2 == 0 else ()), tag="xbTst")

        for el0 in range(5):
            emit_gather(el0)
        def emit_ldew2(el):
            esl = el % 2
            dma("scalar", ew2_db[:, esl * 6 * D:(esl + 1) * 6 * D],
                ew2b[el].rearrange("(k p) dd -> p k dd", p=128),
                outs=[ew2_s[esl]],
                extra=(fence_x1T + c.fence(V("pcast")) if el == 0 else ()), tag="ldew2",
                sem="q_ew2s%d" % esl)

        def emit_ldew13(el, ij):
            w = 128 if ij < 5 else 64
            jp = ij % 2
            dma("scalar", ew13_db[:, jp * 2048:jp * 2048 + 2 * w * 8],
                ew13b[el][:, ij * 256:ij * 256 + 2 * w].rearrange("(k p) cc -> p k cc", p=128),
                outs=[ew13_s[jp]],
                extra=(fence_x1T + c.fence(V("pcast")) if el == 0 and ij < 2 else ()),
                tag="ldew13", sem="q_e13s%d" % jp)

        emit_ldew2(0)
        emit_ldew13(0, 0)
        emit_ldew13(0, 1)
        for el in range(EPC):
            esl = el % 2
            sl5 = el % 5
            for ij in range(6):
                w = 128 if ij < 5 else 64
                jp = ij % 2
                for half in range(2):
                    pa = pm1 if half == 0 else pm2
                    pb = (pm1hB if half == 0 else pm2hB)[jp]
                    for kc in range(8):
                        c.op("tensor", lambda e, k=kc, j=ij, hf=half, w_=w, p=pa, t5=sl5: e.matmul(
                            p[0:w_, (j % 2) * 512:(j % 2) * 512 + 512],
                            ew13_db[:, (j % 2) * 2048 + k * 2 * w_ + hf * w_:
                                    (j % 2) * 2048 + k * 2 * w_ + hf * w_ + w_],
                            xbTa[t5][:, k * CAP: k * CAP + 512],
                            start=(k == 0), stop=(k == 7)),
                            ins=[ew13_s[jp], xbT_b[sl5]], outs=[pb],
                            extra=(c.fence(pmB[0], pmB[1]) if el == 0 and ij == 0 and half == 0 and kc == 0 else ()),
                            tag="e13")
                if ij < 4:
                    emit_ldew13(el, ij + 2)
                elif ij == 4:
                    if el + 1 < EPC:
                        emit_ldew2(el + 1)
                else:
                    if el + 1 < EPC:
                        emit_ldew13(el + 1, 0)
                        emit_ldew13(el + 1, 1)
                c.op("scalar", lambda e, w_=w, j=jp: e.activation(
                    shsilu[0:w_, 0:512], pm1[0:w_, j * 512:j * 512 + 512], AF.Silu),
                    ins=[pm1hB[jp]], outs=[V("shsilu")], tag="esilu")
                c.op("vector", lambda e, j=ij, w_=w, jp_=jp: e.tensor_mul(
                    hT[0:w_, j * 512:(j + 1) * 512], shsilu[0:w_, 0:512], pm2[0:w_, jp_ * 512:jp_ * 512 + 512]),
                    ins=[V("shsilu"), pm2hB[jp]], outs=[hT_b], tag="emul")
            if el + 5 < EPC:
                emit_gather(el + 5)
            for sc2 in range(4):
                pa = pm1 if sc2 % 2 == 0 else pm2
                phB = pm1hB if sc2 % 2 == 0 else pm2hB
                for h2 in range(2):
                    for ik in range(6):
                        c.op("tensor", lambda e, i=ik, s=sc2, sl_=esl, hh2=h2, p=pa: e.matmul(
                            p[:, hh2 * 512:hh2 * 512 + 512],
                            hT[:, i * 512 + s * 128: i * 512 + s * 128 + 128],
                            ew2_db[:, sl_ * 6 * D + i * D + hh2 * 512: sl_ * 6 * D + i * D + hh2 * 512 + 512],
                            start=(i == 0), stop=(i == 5)),
                            ins=[ew2_s[esl], hT_b, V("hTz")], outs=[phB[h2]], tag="e2")
                bybs = ybs_b[sc2 % 2]
                c.op("vector", lambda e, p=pa, s=sc2: e.tensor_copy(
                    ybs[:, (s % 2) * 1024:(s % 2) * 1024 + 1024], p[:, 0:1024]),
                    ins=[phB[0], phB[1]], outs=[bybs],
                    extra=(c.fence(wsh_b[0], wsh_b[1]) if el == 0 and sc2 < 2 else ()),
                    tag="ybevac")
                dma("sync", yb_mine[el * CAP + sc2 * 128: el * CAP + sc2 * 128 + 128, :],
                    ybs[:, (sc2 % 2) * 1024:(sc2 % 2) * 1024 + 1024],
                    ins=[bybs], outs=[ybm_b2[ych_start[el]]], tag="ybst",
                    sem="q_ybst%d" % ych_start[el])
            if el in (1, 5, 7):
                ce = ych_start[el]
                r0 = (0, 2, 6)[ce] * 512
                r1 = (2, 6, 8)[ce] * 512
                c.op("gpsimd", lambda e, a=r0, b=r1: e.collective_compute(
                    "AllGather", ALU.bypass, replica_groups=[list(range(NC_))],
                    ins=[yb_mine[a:b, :]],
                    outs=[yb_all[a * 8:b * 8, :]]),
                    ins=[ybm_b2[ce]], outs=[V("yb_all")], inc=1, tag="agyb")

        if tap == "expert":
            dma("scalar", taps["t_yb"], yb_mine, ins=list(ybm_b2), outs=[V("tapyb")], tag="tape")
        yba_b = V("yb_all")

        _maybe_stop("EX")
        gb_b = [V(f"gb{g}") for g in range(4)]
        fence_ex = c.fence(*ew2_s, *ew13_s)
        for g in range(4):
            c.op("vector", lambda e, gg=g: e.memset(gbuf[:, gg * 1024:(gg + 1) * 1024], 0.0),
                 outs=[gb_b[g]], extra=fence_ex, tag="gbz")
        out_b = V("out")
        for it in range(NT):
            ya = yacc[:, it * D:(it + 1) * D]
            yab = ytile_b[it]
            for k in range(TOPK):
                g = k % 4
                c.op("gpsimd", lambda e, t=it, kk=k, gg=g: e.indirect_dma_start(
                    out=gbuf[:, gg * 1024:(gg + 1) * 1024], out_offset=None,
                    in_=yb_all, in_offset=IndirectOffsetOnAxis(raddr2[:, t * 8 + kk:t * 8 + kk + 1], 0),
                    bounds_check=bcr(e, E * CAP - 1), oob_is_err=False),
                    ins=[yba_b, V(f"raddr2_{it}")], outs=[gb_b[g]], inc=16,
                    tag="gyb", sem="q_gyb%d" % g)
                c.op("vector", lambda e, t=it, kk=k, gg=g, y=ya: e.scalar_tensor_tensor(
                    y, gbuf[:, gg * 1024:(gg + 1) * 1024], rwk[:, t * 8 + kk:t * 8 + kk + 1], y,
                    op0=ALU.mult, op1=ALU.add),
                    ins=[gb_b[g], rwk_b[it]], outs=[yab], tag="mac")
            plain_ln(ya, yab, "ln2", par=it % 2)
            dma("scalar", out[it * 128:(it + 1) * 128, :], ya, ins=[yab], outs=[out_b], tag="outst")

    except _StopBuild:
        pass
    if not V("out").w:
        dma("scalar", out[0:128, :], wk[:, 0:1024], tag="outst", outs=[V("out")])
    out_b = V("out")
    for t in out_b.w:
        c._wait("scalar", t.sem, t.val)

    from contextlib import ExitStack
    es = ExitStack()
    with es:
        sems = {}
        for e in sorted(set(list(ENGINES) + list(c.cnt.keys()))):
            sems[e] = es.enter_context(nc.semaphore(f"sem_{e}"))
        block = es.enter_context(nc.Block())
        block.sync(lambda p: c.replay("sync", p, sems))
        block.scalar(lambda p: c.replay("scalar", p, sems))
        block.vector(lambda p: c.replay("vector", p, sems))
        block.tensor(lambda p: c.replay("tensor", p, sems))
        block.gpsimd(lambda p: c.replay("gpsimd", p, sems))
    for e in ENGINES:
        assert c.cnt[e] < 60000, (e, c.cnt[e])
    return nc


def host_prep(inputs):
    import ml_dtypes
    x = np.ascontiguousarray(np.asarray(inputs["L0_attn_x"], dtype=np.float32))
    wqkv = np.asarray(inputs["L0_attn_w_qkv"], dtype=np.float32)
    wout = np.ascontiguousarray(np.asarray(inputs["L0_attn_w_out"], dtype=np.float32))
    bout = np.asarray(inputs["L0_attn_b_out"], dtype=np.float32)
    gamma = np.asarray(inputs["L0_attn_ln_gamma"], dtype=np.float32)
    beta = np.asarray(inputs["L0_attn_ln_beta"], dtype=np.float32)
    gate_w = np.asarray(inputs["L0_ffn_gate_w"], dtype=np.float32)
    gate_b = np.asarray(inputs["L0_ffn_gate_b"], dtype=np.float32)
    w1 = np.asarray(inputs["L0_ffn_w1"], dtype=np.float32)
    w2 = np.asarray(inputs["L0_ffn_w2"], dtype=np.float32)
    w3 = np.asarray(inputs["L0_ffn_w3"], dtype=np.float32)
    ws1 = np.asarray(inputs["L0_ffn_ws1"], dtype=np.float32)
    ws2 = np.asarray(inputs["L0_ffn_ws2"], dtype=np.float32)
    ws3 = np.asarray(inputs["L0_ffn_ws3"], dtype=np.float32)

    wqkv_p = np.ascontiguousarray(wqkv)

    boutc = np.ascontiguousarray(bout.reshape(8, 128).T)

    ws13m = np.empty((D, 2 * SI), np.float32)
    w1T, w3T = ws1.T, ws3.T
    for j in range(SI // 128):
        ws13m[:, j * 256:j * 256 + 128] = w1T[:, j * 128:(j + 1) * 128]
        ws13m[:, j * 256 + 128:j * 256 + 256] = w3T[:, j * 128:(j + 1) * 128]
    ws2T = np.ascontiguousarray(ws2.T)

    ew13_all = np.empty((E, D, 2 * I_FF), np.float32)
    w1t = np.transpose(w1, (0, 2, 1))
    w3t = np.transpose(w3, (0, 2, 1))
    for j in range(6):
        w = 128 if j < 5 else 64
        ew13_all[:, :, j * 256:j * 256 + w] = w1t[:, :, j * 128:j * 128 + w]
        ew13_all[:, :, j * 256 + w:j * 256 + 2 * w] = w3t[:, :, j * 128:j * 128 + w]
    ew2T_all = np.zeros((E, 768, D), np.float32)
    ew2T_all[:, :I_FF, :] = np.transpose(w2, (0, 2, 1))

    inv_freq = (1.0 / (10000.0 ** (np.arange(0, HD, 2, dtype=np.float32) / np.float32(HD)))).astype(np.float32)
    pos = np.arange(N, dtype=np.float32)
    ang = pos[:, None] * inv_freq[None, :]
    cs = np.cos(ang).astype(np.float32).T
    sn = np.sin(ang).astype(np.float32).T
    cosf = np.ascontiguousarray(np.tile(cs, (4, 1)))
    sins = np.ascontiguousarray(np.tile(np.concatenate([-sn, sn], 0), (2, 1)))
    permf = np.zeros((128, 128), np.float32)
    for m in range(128):
        permf[(m + 32) % 64 + 64 * (m // 64), m] = 1.0

    shared = dict(
        wqkv=wqkv_p, wout=wout, boutc=boutc,
        gamma=np.ascontiguousarray(gamma.reshape(1, D)),
        beta=np.ascontiguousarray(beta.reshape(1, D)),
        gwT=np.ascontiguousarray(gate_w.T),
        gateb=np.ascontiguousarray(gate_b.reshape(1, E)),
        ws13=ws13m, ws2T=ws2T, cosf=cosf, sins=sins,
        identb=np.eye(128, dtype=np.float32).astype(ml_dtypes.bfloat16),
        permf=permf,
        identf=np.eye(128, dtype=np.float32),
        trix=np.triu(np.ones((128, 128), np.float32), 1),
        onesr=np.ones((1, 128), np.float32),
        onesc=np.ones((128, 1), np.float32),
        iota64=np.tile(np.arange(E, dtype=np.float32), (128, 1)),
        gbase=np.tile(np.array([
            [0, 8192, 24576][(0 if e % 8 < 2 else (1 if e % 8 < 6 else 2))]
            + (e // 8) * [1024, 2048, 1024][(0 if e % 8 < 2 else (1 if e % 8 < 6 else 2))]
            + (e % 8 - [0, 2, 6][(0 if e % 8 < 2 else (1 if e % 8 < 6 else 2))]) * 512
            for e in range(E)], dtype=np.float32), (128, 1)),
    )
    in_maps = []
    p_ar = np.arange(128, dtype=np.float32)
    for cc in range(NC_):
        selm = np.zeros((128, 16), np.float32)
        for j in range(16):
            selm[cc * 16 + j, j] = 1.0
        tokid_m = np.empty((128, NT), np.float32)
        for j in range(NT):
            tokid_m[:, j] = cc * N + j * 128 + p_ar + 2.0
        slotrow_m = (cc * 32 + np.arange(32, dtype=np.uint32)).reshape(32, 1)
        m = dict(shared)
        m["x"] = np.ascontiguousarray(x[cc])
        m["ew13"] = np.ascontiguousarray(ew13_all[cc * EPC:(cc + 1) * EPC])
        m["ew2T"] = np.ascontiguousarray(ew2T_all[cc * EPC:(cc + 1) * EPC])
        m["sel"] = selm
        m["tokid"] = tokid_m
        m["slotrow"] = slotrow_m
        in_maps.append(m)
    return in_maps


_CACHE = {}


def kernel(**inputs):
    in_maps = host_prep(inputs)
    if "nc" not in _CACHE:
        _CACHE["nc"] = build_program()
    nc = _CACHE["nc"]
    res = run_bass_kernel_spmd(nc, in_maps, core_ids=list(range(NC_)))
    return np.stack([res.results[cc]["out"] for cc in range(NC_)], axis=0)
